# version 15
# baseline (speedup 1.0000x reference)
import numpy as np
import ml_dtypes
import concourse.bass as bass
import concourse.mybir as mybir
from concourse.bass_utils import run_bass_kernel_spmd
from contextlib import ExitStack

F32 = mybir.dt.float32
BF16 = mybir.dt.bfloat16
I32 = mybir.dt.int32
AF = mybir.ActivationFunctionType
ALU = mybir.AluOpType
AX = mybir.AxisListType

D = 2048
DC = 16
NT = 4096
TT = 32
H = 16
HID = 5632
HC = 44
CH = 32
GR = 8
EPS = 1e-5
N_CORES = 8


class Sem:
    def __init__(self, handle):
        self.h = handle
        self.n = 0

    def inc(self, instr, k=1):
        instr.then_inc(self.h, k)
        self.n += k
        return self.n


def build_program(phases=("P1", "P2", "P3", "P5", "P4", "P55", "P6"), dbg=()):
    nc = bass.Bass("TRN2", target_bir_lowering=False)

    def scratch(name, shape, dt):
        kind = "ExternalOutput" if name in dbg else "Internal"
        return nc.dram_tensor(name, shape, dt, kind=kind).ap()

    x_in = nc.dram_tensor("x", [NT, D], F32, kind="ExternalInput").ap()
    wqkv = nc.dram_tensor("wqkv", [D, 3 * D], F32, kind="ExternalInput").ap()
    bqkv_c = nc.dram_tensor("bqkv_c", [128, 48], F32, kind="ExternalInput").ap()
    wout = nc.dram_tensor("wout", [D, D], F32, kind="ExternalInput").ap()
    bout_r = nc.dram_tensor("bout_r", [1, D], F32, kind="ExternalInput").ap()
    slope_b = nc.dram_tensor("slope_b", [128, H], F32, kind="ExternalInput").ap()
    wgT = nc.dram_tensor("wgT", [D, HID], I32, kind="ExternalInput").ap()
    wuT = nc.dram_tensor("wuT", [D, HID], I32, kind="ExternalInput").ap()
    wdT = nc.dram_tensor("wdT", [HID, D], I32, kind="ExternalInput").ap()
    sg_c = nc.dram_tensor("sg_c", [128, HC], F32, kind="ExternalInput").ap()
    su_c = nc.dram_tensor("su_c", [128, HC], F32, kind="ExternalInput").ap()
    sd_c = nc.dram_tensor("sd_c", [128, DC], F32, kind="ExternalInput").ap()
    ident = nc.dram_tensor("ident", [128, 128], BF16, kind="ExternalInput").ap()
    ones_b = nc.dram_tensor("ones_b", [1, 128], BF16, kind="ExternalInput").ap()
    ones_f = nc.dram_tensor("ones_f", [1, 128], F32, kind="ExternalInput").ap()
    bqv_row = nc.dram_tensor("bqv_row", [1, D], F32, kind="ExternalInput").ap()
    selm = nc.dram_tensor("selm", [128, N_CORES], F32, kind="ExternalInput").ap()
    out = nc.dram_tensor("out", [NT, D], F32, kind="ExternalOutput").ap()

    pad0 = nc.dram_tensor("pad0", [4096, 1024], F32).ap()
    cc_in = nc.dram_tensor("cc_in", [H * 128, 128], F32).ap()
    pad1 = nc.dram_tensor("pad1", [4096, 1024], F32).ap()
    cc_out = nc.dram_tensor("cc_out", [N_CORES * H * 128, 128], F32).ap()
    pad2 = nc.dram_tensor("pad2", [4096, 1024], F32).ap()
    q_s = scratch("q_s", [H, 128, NT], BF16)
    k_s = scratch("k_s", [H, 128, NT], BF16)
    v_s2 = scratch("v_s2", [NT, D], BF16)
    gu_s = scratch("gu_s", [HC, 128, NT], BF16)
    a_s = scratch("a_s", [H, 128, NT], BF16)
    g_tm = scratch("g_tm", [NT, D], BF16)
    s_dbg = scratch("s_dbg", [NT, D], F32)

    es = ExitStack()

    def sb(name, shape, dt):
        return es.enter_context(nc.sbuf_tensor(name, shape, dt))

    def sem(name):
        return Sem(es.enter_context(nc.semaphore(name)))

    with es:
        sid = sb("sid", [128, 128], BF16)
        sones = sb("sones", [1, 128], BF16)
        sonesf = sb("sonesf", [1, 128], F32)
        sbq = sb("sbq", [128, 48], F32)
        sbout_b = sb("sbout_b", [1, D], BF16)
        sslope = sb("sslope", [128, H], F32)
        ssg = sb("ssg", [128, HC], F32)
        ssu = sb("ssu", [128, HC], F32)
        ssd = sb("ssd", [128, DC], F32)
        ssel = sb("ssel", [128, N_CORES], F32)
        S_f = sb("S_f", [128, H * 128], F32)
        S_b = sb("S_b", [128, H * 128], BF16)

        LD = sem("LD")
        ST = sem("ST")
        CC = sem("CC")
        sA = sem("sA")
        sB = sem("sB")
        sC = sem("sC")
        sD = sem("sD")
        sV = sem("sV")
        sW = sem("sW")
        sP = sem("sP")
        sQ = sem("sQ")
        sG = sem("sG")

        def fence(name, cc=False):
            with nc.Block(name) as blk:
                @blk.gpsimd
                def _(gp):
                    gp.wait_ge(ST.h, ST.n)
                    if cc:
                        gp.wait_ge(CC.h, CC.n)
                @blk.sync
                def _(sy):
                    sy.wait_ge(LD.h, LD.n)

        with ExitStack() as ps0:
            sbout = ps0.enter_context(nc.sbuf_tensor("sbout", [1, D], F32))
            with nc.Block("setup") as blk:
                @blk.sync
                def _(sy):
                    sy.dma_start(out=sid[:], in_=ident).then_inc(LD.h, 16)
                    sy.dma_start(out=sones[:], in_=ones_b).then_inc(LD.h, 16)
                    sy.dma_start(out=sonesf[:], in_=ones_f).then_inc(LD.h, 16)
                    sy.dma_start(out=sbq[:], in_=bqkv_c).then_inc(LD.h, 16)
                    sy.dma_start(out=sbout[:], in_=bout_r).then_inc(LD.h, 16)
                    sy.dma_start(out=sslope[:], in_=slope_b).then_inc(LD.h, 16)
                    sy.dma_start(out=ssg[:], in_=sg_c).then_inc(LD.h, 16)
                    sy.dma_start(out=ssu[:], in_=su_c).then_inc(LD.h, 16)
                    sy.dma_start(out=ssd[:], in_=sd_c).then_inc(LD.h, 16)
                    sy.dma_start(out=ssel[:], in_=selm).then_inc(LD.h, 16)
                LD.n += 16 * 10

                @blk.scalar
                def _(sc):
                    sc.wait_ge(LD.h, LD.n)
                    ins = sc.activation(out=sbout_b[:], in_=sbout[:], func=AF.Copy)
                    sA.inc(ins)

        es_xn = ExitStack()
        xn_fm = es_xn.enter_context(nc.sbuf_tensor("xn_fm", [128, DC * NT], BF16))

        if "P1" in phases:
            with ExitStack() as ps:
                x_st = [ps.enter_context(nc.sbuf_tensor(f"p1x{i}", [128, D], F32)) for i in range(2)]
                xsq = ps.enter_context(nc.sbuf_tensor("p1sq", [128, D], F32))
                xn_tm = [ps.enter_context(nc.sbuf_tensor(f"p1n{i}", [128, D], BF16)) for i in range(2)]
                sums = ps.enter_context(nc.sbuf_tensor("p1sums", [128, TT], F32))
                ssqs = ps.enter_context(nc.sbuf_tensor("p1ssqs", [128, TT], F32))
                mean = ps.enter_context(nc.sbuf_tensor("p1mean", [128, TT], F32))
                e2 = ps.enter_context(nc.sbuf_tensor("p1e2", [128, TT], F32))
                msq = ps.enter_context(nc.sbuf_tensor("p1msq", [128, TT], F32))
                rstd = ps.enter_context(nc.sbuf_tensor("p1rs", [128, TT], F32))
                nmr = ps.enter_context(nc.sbuf_tensor("p1nm", [128, TT], F32))
                pT = [ps.enter_context(nc.psum_tensor(f"p1p{i}", [128, D], BF16)) for i in range(2)]

                ld0 = LD.n
                a0, v0, w0, d0, c0, b0, p0, q0 = sA.n, sV.n, sW.n, sD.n, sC.n, sB.n, sP.n, sQ.n
                with nc.Block("P1") as blk:
                    @blk.sync
                    def _(sy):
                        for t in range(TT):
                            if t >= 2:
                                sy.wait_ge(sB.h, b0 + t - 1)
                            sy.dma_start(out=x_st[t % 2][:],
                                         in_=x_in[t * 128:(t + 1) * 128, :]).then_inc(LD.h, 16)
                    LD.n += 16 * TT

                    @blk.scalar
                    def _(sc):
                        for t in range(TT):
                            sc.wait_ge(LD.h, ld0 + 16 * (t + 1))
                            ins = sc.activation(out=xsq[:], in_=x_st[t % 2][:], func=AF.Square,
                                                accum_out=ssqs[:, t:t + 1])
                            sA.inc(ins)
                            sc.wait_ge(sW.h, w0 + t + 1)
                            sc.activation(out=e2[:, t:t + 1], in_=e2[:, t:t + 1],
                                          func=AF.Ln)
                            ins = sc.activation(out=rstd[:, t:t + 1], in_=e2[:, t:t + 1],
                                                func=AF.Exp, scale=-0.5)
                            sD.inc(ins)
                            sc.wait_ge(sC.h, c0 + t + 1)
                            if t >= 2:
                                sc.wait_ge(sP.h, p0 + t - 1)
                            ins = sc.activation(out=xn_tm[t % 2][:], in_=x_st[t % 2][:],
                                                func=AF.Identity, scale=rstd[:, t:t + 1],
                                                bias=nmr[:, t:t + 1])
                            sB.inc(ins)

                    @blk.vector
                    def _(ve):
                        for t in range(TT):
                            ve.wait_ge(LD.h, ld0 + 16 * (t + 1))
                            ins = ve.reduce_sum(sums[:, t:t + 1], x_st[t % 2][:], axis=AX.X)
                            sV.inc(ins)
                            ve.wait_ge(sA.h, a0 + t + 1)
                            ve.tensor_scalar_mul(mean[:, t:t + 1], sums[:, t:t + 1], 1.0 / D)
                            ve.tensor_scalar_mul(e2[:, t:t + 1], ssqs[:, t:t + 1], 1.0 / D)
                            ve.tensor_mul(msq[:, t:t + 1], mean[:, t:t + 1], mean[:, t:t + 1])
                            ve.tensor_sub(e2[:, t:t + 1], e2[:, t:t + 1], msq[:, t:t + 1])
                            ins = ve.tensor_scalar_add(e2[:, t:t + 1], e2[:, t:t + 1], EPS)
                            sW.inc(ins)
                            ve.wait_ge(sD.h, d0 + t + 1)
                            ins = ve.scalar_tensor_tensor(nmr[:, t:t + 1], mean[:, t:t + 1],
                                                          -1.0, rstd[:, t:t + 1],
                                                          ALU.mult, ALU.mult)
                            sC.inc(ins)
                            if t >= 1:
                                ve.wait_ge(sP.h, p0 + t)
                                ins = ve.tensor_copy(
                                    xn_fm[:].rearrange("p (c n) -> p c n", c=DC)[
                                        :, :, (t - 1) * 128:t * 128],
                                    pT[(t - 1) % 2][:].rearrange("p (c i) -> p c i", c=DC))
                                sQ.inc(ins)
                        ve.wait_ge(sP.h, p0 + TT)
                        ins = ve.tensor_copy(
                            xn_fm[:].rearrange("p (c n) -> p c n", c=DC)[
                                :, :, (TT - 1) * 128:TT * 128],
                            pT[(TT - 1) % 2][:].rearrange("p (c i) -> p c i", c=DC))
                        sQ.inc(ins)

                    @blk.tensor
                    def _(te):
                        for t in range(TT):
                            te.wait_ge(sB.h, b0 + t + 1)
                            if t >= 2:
                                te.wait_ge(sQ.h, q0 + t - 1)
                            for d in range(DC):
                                ins = te.transpose(pT[t % 2][:, d * 128:(d + 1) * 128],
                                                   xn_tm[t % 2][:, d * 128:(d + 1) * 128],
                                                   sid[:])
                            sP.inc(ins)

        if "P1" in phases:
            fence("f1")

        if "P2" in phases:
            with ExitStack() as ps:
                wbf = [ps.enter_context(nc.sbuf_tensor(f"p2b{i}", [128, D], BF16)) for i in range(2)]
                ev = [ps.enter_context(nc.sbuf_tensor(f"p2e{i}", [128, 512], BF16)) for i in range(4)]
                pQ = [ps.enter_context(nc.psum_tensor(f"p2p{i}", [128, 512], F32)) for i in range(4)]

                ld0, st0, a0, b0, p0 = LD.n, ST.n, sA.n, sB.n, sP.n
                with nc.Block("P2a") as blk:
                    @blk.gpsimd
                    def _(gp):
                        for cg in range(32):
                            if cg >= 2:
                                gp.wait_ge(sP.h, p0 + 8 * (cg - 1))
                            gp.dma_start(
                                out=wbf[cg % 2][:].rearrange("p (c m) -> p c m", c=DC),
                                in_=wqkv[:, cg * 128:(cg + 1) * 128].rearrange(
                                    "(c p) m -> p c m", p=128),
                            ).then_inc(LD.h, 16)
                    LD.n += 16 * 32

                    @blk.scalar
                    def _(sc):
                        for cg in range(32):
                            for g in range(8):
                                k = cg * 8 + g
                                sc.wait_ge(sP.h, p0 + k + 1)
                                if k >= 4:
                                    sc.wait_ge(ST.h, st0 + 16 * (k - 3))
                                ins = sc.activation(out=ev[k % 4][:], in_=pQ[k % 4][:],
                                                    func=AF.Identity, bias=sbq[:, cg:cg + 1])
                                sB.inc(ins)

                    @blk.tensor
                    def _(te):
                        for cg in range(32):
                            te.wait_ge(LD.h, ld0 + 16 * (cg + 1))
                            for g in range(8):
                                k = cg * 8 + g
                                if k >= 4:
                                    te.wait_ge(sB.h, b0 + k - 3)
                                for d in range(DC):
                                    ins = te.matmul(
                                        pQ[k % 4][:],
                                        wbf[cg % 2][:, d * 128:(d + 1) * 128],
                                        xn_fm[:, d * NT + g * 512: d * NT + g * 512 + 512],
                                        start=(d == 0), stop=(d == DC - 1))
                                sP.inc(ins)

                    @blk.sync
                    def _(sy):
                        for cg in range(32):
                            typ, h = divmod(cg, H)
                            dst = (q_s, k_s)[typ]
                            for g in range(8):
                                k = cg * 8 + g
                                sy.wait_ge(sB.h, b0 + k + 1)
                                sy.dma_start(out=dst[h, :, g * 512:(g + 1) * 512],
                                             in_=ev[k % 4][:]).then_inc(ST.h, 16)
                    ST.n += 16 * 8 * 32

        if "P2" in phases:
            fence("f2a")

        if "P2" in phases:
            with ExitStack() as ps:
                wv_bf = [ps.enter_context(nc.sbuf_tensor(f"p2vb{i}", [128, 16 * 512], BF16)) for i in range(2)]
                bqv_r = ps.enter_context(nc.sbuf_tensor("p2bqr", [1, D], F32))
                bqv = ps.enter_context(nc.sbuf_tensor("p2bqv", [128, D], F32))
                ev2 = [ps.enter_context(nc.sbuf_tensor(f"p2e2{i}", [128, 512], BF16)) for i in range(4)]
                pQ2 = [ps.enter_context(nc.psum_tensor(f"p2q2{i}", [128, 512], F32)) for i in range(4)]

                ld0, st0, a0, p0, q0, v0, w0 = LD.n, ST.n, sA.n, sP.n, sQ.n, sV.n, sW.n
                with nc.Block("P2b") as blk:
                    @blk.sync
                    def _(sy):
                        sy.dma_start(out=bqv_r[:], in_=bqv_row).then_inc(LD.h, 16)
                        for vg in range(4):
                            for tt in range(TT):
                                k = vg * TT + tt
                                sy.wait_ge(sW.h, w0 + k + 1)
                                sy.dma_start(
                                    out=v_s2[tt * 128:(tt + 1) * 128,
                                             vg * 512:(vg + 1) * 512],
                                    in_=ev2[k % 4][:]).then_inc(ST.h, 16)
                    ST.n += 16 * 128

                    @blk.gpsimd
                    def _(gp):
                        for vg in range(4):
                            if vg >= 2:
                                gp.wait_ge(sP.h, p0 + TT * (vg - 1))
                            ins = gp.dma_start(
                                out=wv_bf[vg % 2][:].rearrange("p (c n) -> p c n", c=DC),
                                in_=wqkv[:, 2 * D + vg * 512: 2 * D + (vg + 1) * 512
                                         ].rearrange("(c p) n -> p c n", p=128))
                            ins.then_inc(LD.h, 16)
                    LD.n += 16 * 5

                    @blk.tensor
                    def _(te):
                        te.wait_ge(LD.h, ld0 + 16)
                        for i in range(4):
                            ins = te.matmul(pQ2[i][:], sonesf[:, 0:128],
                                            bqv_r[:, i * 512:(i + 1) * 512],
                                            start=True, stop=True)
                        sQ.inc(ins)
                        te.wait_ge(sV.h, v0 + 1)
                        for vg in range(4):
                            te.wait_ge(LD.h, ld0 + 16 * (vg + 2))
                            for tt in range(TT):
                                k = vg * TT + tt
                                if k >= 4:
                                    te.wait_ge(sW.h, w0 + k - 3)
                                for d in range(DC):
                                    ins = te.matmul(
                                        pQ2[k % 4][:],
                                        xn_fm[:, d * NT + tt * 128: d * NT + tt * 128 + 128],
                                        wv_bf[vg % 2][:, d * 512:(d + 1) * 512],
                                        start=(d == 0), stop=(d == DC - 1))
                                sP.inc(ins)

                    @blk.vector
                    def _(ve):
                        ve.wait_ge(sQ.h, q0 + 1)
                        for i in range(4):
                            ve.tensor_copy(bqv[:, i * 512:(i + 1) * 512], pQ2[i][:])
                        sV.inc(ve.nop())
                        for vg in range(4):
                            for tt in range(TT):
                                k = vg * TT + tt
                                ve.wait_ge(sP.h, p0 + k + 1)
                                if k >= 4:
                                    ve.wait_ge(ST.h, st0 + 16 * (k - 3))
                                ins = ve.tensor_add(ev2[k % 4][:], pQ2[k % 4][:],
                                                    bqv[:, vg * 512:(vg + 1) * 512])
                                sW.inc(ins)

        if "P2" in phases:
            fence("f2")

        if "P3" in phases:
            with ExitStack() as ps:
                wbfs = [ps.enter_context(nc.sbuf_tensor(f"p3b{i}", [128, D], BF16)) for i in range(4)]
                silu_t = [ps.enter_context(nc.sbuf_tensor(f"p3s{i}", [128, 512], F32)) for i in range(2)]
                up_t = [ps.enter_context(nc.sbuf_tensor(f"p3u{i}", [128, 512], F32)) for i in range(2)]
                gu_t = [ps.enter_context(nc.sbuf_tensor(f"p3g{i}", [128, 512], BF16)) for i in range(4)]
                pG = [ps.enter_context(nc.psum_tensor(f"p3pg{i}", [128, 512], F32)) for i in range(2)]
                pU = [ps.enter_context(nc.psum_tensor(f"p3pu{i}", [128, 512], F32)) for i in range(2)]

                ld0, st0, a0, b0, p0, v0, w0 = LD.n, ST.n, sA.n, sB.n, sP.n, sV.n, sW.n
                with nc.Block("P3") as blk:
                    @blk.gpsimd
                    def _(gp):
                        for hc in range(HC):
                            if hc >= 2:
                                gp.wait_ge(sP.h, p0 + 8 * (hc - 1))
                            gp.dma_start(
                                out=wbfs[(2 * hc) % 4][:].rearrange("p (c h) -> p c h", c=DC),
                                in_=wgT[:, hc * 128:(hc + 1) * 128].rearrange(
                                    "(c p) h -> p c h", p=128)).then_inc(LD.h, 16)
                            gp.dma_start(
                                out=wbfs[(2 * hc + 1) % 4][:].rearrange("p (c h) -> p c h", c=DC),
                                in_=wuT[:, hc * 128:(hc + 1) * 128].rearrange(
                                    "(c p) h -> p c h", p=128)).then_inc(LD.h, 16)
                    LD.n += 32 * HC

                    @blk.scalar
                    def _(sc):
                        def silus(hc):
                            for g in range(8):
                                k = hc * 8 + g
                                sc.wait_ge(sP.h, p0 + k + 1)
                                if k >= 2:
                                    sc.wait_ge(sW.h, w0 + k - 1)
                                ins = sc.activation(out=silu_t[k % 2][:], in_=pG[k % 2][:],
                                                    func=AF.Silu, scale=ssg[:, hc:hc + 1])
                                sB.inc(ins)
                        for hc in range(HC):
                            sc.wait_ge(LD.h, ld0 + 16 * (2 * hc + 2))
                            ins = sc.activation(out=wbfs[(2 * hc) % 4][:],
                                                in_=wbfs[(2 * hc) % 4][:],
                                                func=AF.Copy, bias=-128.0)
                            sA.inc(ins)
                            ins = sc.activation(out=wbfs[(2 * hc + 1) % 4][:],
                                                in_=wbfs[(2 * hc + 1) % 4][:],
                                                func=AF.Copy, bias=-128.0)
                            sA.inc(ins)
                            if hc >= 1:
                                silus(hc - 1)
                        silus(HC - 1)

                    @blk.tensor
                    def _(te):
                        for hc in range(HC):
                            te.wait_ge(sA.h, a0 + 2 * hc + 2)
                            for g in range(8):
                                k = hc * 8 + g
                                if k >= 2:
                                    te.wait_ge(sB.h, b0 + k - 1)
                                    te.wait_ge(sV.h, v0 + k - 1)
                                for d in range(DC):
                                    te.ldweights(wbfs[(2 * hc) % 4][:, ((d + 1) % DC) * 128:
                                                                      ((d + 1) % DC) * 128 + 128])
                                    te.matmul(pG[k % 2][:],
                                              wbfs[(2 * hc) % 4][:, d * 128:(d + 1) * 128],
                                              xn_fm[:, d * NT + g * 512: d * NT + g * 512 + 512],
                                              start=(d == 0), stop=(d == DC - 1))
                                for d in range(DC):
                                    ins = te.matmul(
                                        pU[k % 2][:],
                                        wbfs[(2 * hc + 1) % 4][:, d * 128:(d + 1) * 128],
                                        xn_fm[:, d * NT + g * 512: d * NT + g * 512 + 512],
                                        start=(d == 0), stop=(d == DC - 1))
                                sP.inc(ins)

                    @blk.vector
                    def _(ve):
                        for hc in range(HC):
                            for g in range(8):
                                k = hc * 8 + g
                                ve.wait_ge(sP.h, p0 + k + 1)
                                if k >= 2:
                                    ve.wait_ge(sW.h, w0 + k - 1)
                                ins = ve.tensor_scalar(up_t[k % 2][:], pU[k % 2][:],
                                                       ssu[:, hc:hc + 1], None, ALU.mult)
                                sV.inc(ins)
                                ve.wait_ge(sB.h, b0 + k + 1)
                                if k >= 4:
                                    ve.wait_ge(ST.h, st0 + 16 * (k - 3))
                                ins = ve.tensor_mul(gu_t[k % 4][:], silu_t[k % 2][:],
                                                    up_t[k % 2][:])
                                sW.inc(ins)

                    @blk.sync
                    def _(sy):
                        for hc in range(HC):
                            for g in range(8):
                                k = hc * 8 + g
                                sy.wait_ge(sW.h, w0 + k + 1)
                                sy.dma_start(out=gu_s[hc, :, g * 512:(g + 1) * 512],
                                             in_=gu_t[k % 4][:]).then_inc(ST.h, 16)
                    ST.n += 16 * 8 * HC

        if "P3" in phases:
            fence("f3")

        es_xn.close()

        if "P5" in phases:
            with ExitStack() as ps:
                mask_a = ps.enter_context(nc.sbuf_tensor("mask_a", [128, H * 128], BF16))
                qdec_s = ps.enter_context(nc.sbuf_tensor("qdec_s", [128, H * 128], F32))
                kdec_s = ps.enter_context(nc.sbuf_tensor("kdec_s", [128, H], F32))
                bd_s = ps.enter_context(nc.sbuf_tensor("bd_s", [128, H], F32))
                dtmp = ps.enter_context(nc.sbuf_tensor("dtmp", [128, 32], F32))
                dtmp2 = ps.enter_context(nc.sbuf_tensor("dtmp2", [128, H * 128], F32))

                sq_st = [ps.enter_context(nc.sbuf_tensor(f"p5q{i}", [128, H * 512], BF16)) for i in range(2)]
                sk_st = [ps.enter_context(nc.sbuf_tensor(f"p5k{i}", [128, H * 512], BF16)) for i in range(2)]
                sv_st = [ps.enter_context(nc.sbuf_tensor(f"p5v{i}", [128, 4 * D], BF16)) for i in range(2)]
                kdT = ps.enter_context(nc.sbuf_tensor("p5kd", [128, H * 128], BF16))
                msc = ps.enter_context(nc.sbuf_tensor("p5ms", [128, H * 128], BF16))
                qd_t = ps.enter_context(nc.sbuf_tensor("p5qd", [128, H * 512], BF16))
                o_st = [ps.enter_context(nc.sbuf_tensor(f"p5o{i}", [128, H * 512], BF16)) for i in range(2)]

                pSO = ps.enter_context(nc.psum_tensor("p5so", [128, 2048], F32))
                pT5 = ps.enter_context(nc.psum_tensor("p5t", [128, 2048], BF16))
                pKV = ps.enter_context(nc.psum_tensor("p5kv", [128, 1024], F32))

                slope_v = sslope[:].rearrange("p (h o) -> p h o", o=1).broadcast_to(
                    [128, H, 128])
                g0, a0, v0 = sG.n, sA.n, sV.n
                with nc.Block("P5s") as blk:
                    @blk.gpsimd
                    def _(gp):
                        ins = gp.iota(dtmp2[:].rearrange("p (h i) -> p h i", h=H),
                                      [[0, H], [-1, 128]], base=0, channel_multiplier=1,
                                      allow_small_or_imprecise_dtypes=True)
                        sG.inc(ins)

                    @blk.vector
                    def _(ve):
                        ve.wait_ge(sG.h, g0 + 1)
                        ins = ve.tensor_tensor(dtmp2[:].rearrange("p (h i) -> p h i", h=H),
                                               dtmp2[:].rearrange("p (h i) -> p h i", h=H),
                                               slope_v, ALU.mult)
                        sV.inc(ins)
                        ve.memset(S_f[:], 0.0)
                        ve.memset(S_b[:], 0.0)

                    @blk.scalar
                    def _(sc):
                        sc.wait_ge(sV.h, v0 + 1)
                        ins = sc.activation(out=mask_a[:], in_=dtmp2[:], func=AF.Exp)
                        sA.inc(ins)

                g1, a1, v1 = sG.n, sA.n, sV.n
                with nc.Block("P5s2") as blk:
                    @blk.gpsimd
                    def _(gp):
                        gp.affine_select(
                            mask_a[:].rearrange("p (h i) -> p h i", h=H),
                            mask_a[:].rearrange("p (h i) -> p h i", h=H),
                            [[0, H], [1, 128]], ALU.is_ge, 0.0,
                            base=0, channel_multiplier=-1)
                        gp.iota(dtmp2[:].rearrange("p (h i) -> p h i", h=H),
                                [[0, H], [-1, 128]], base=0, channel_multiplier=0,
                                allow_small_or_imprecise_dtypes=True)
                        ins = gp.iota(dtmp[:, 0:1], [[0, 1]], base=-128, channel_multiplier=1,
                                      allow_small_or_imprecise_dtypes=True)
                        sG.inc(ins)

                    @blk.vector
                    def _(ve):
                        ve.wait_ge(sG.h, g1 + 1)
                        ve.tensor_tensor(dtmp2[:].rearrange("p (h i) -> p h i", h=H),
                                         dtmp2[:].rearrange("p (h i) -> p h i", h=H),
                                         slope_v, ALU.mult)
                        ins = ve.tensor_tensor(dtmp[:, 16:16 + H],
                                               dtmp[:, 0:1].broadcast_to([128, H]),
                                               sslope[:], ALU.mult)
                        sV.inc(ins)

                    @blk.scalar
                    def _(sc):
                        sc.wait_ge(sV.h, v1 + 1)
                        sc.activation(out=qdec_s[:], in_=dtmp2[:], func=AF.Exp)
                        sc.activation(out=kdec_s[:], in_=dtmp[:, 16:16 + H], func=AF.Exp)
                        ins = sc.activation(out=bd_s[:], in_=sslope[:], func=AF.Exp,
                                            scale=-128.0)
                        sA.inc(ins)

                ld0, st0 = LD.n, ST.n
                a0, p0, v0 = sA.n, sP.n, sV.n
                def peb(t):
                    return p0 + 5 * t

                def dvb(t):
                    return v0 + 4 * t + (t // 4 + 1)

                def acb(t):
                    return a0 + 2 * t

                bd_v = bd_s[:].rearrange("p (h o) -> p h o", o=1).broadcast_to([128, H, 128])
                kd_v = kdec_s[:].rearrange("p (h o) -> p h o", o=1).broadcast_to([128, H, 128])
                qd_v = qdec_s[:].rearrange("p (h o i) -> p h o i", h=H, o=1).broadcast_to(
                    [128, H, 4, 128])

                with nc.Block("P5m") as blk:
                    @blk.sync
                    def _(sy):
                        for gr in range(GR):
                            if gr >= 2:
                                sy.wait_ge(sP.h, peb(4 * (gr - 1)))
                            for h in range(H):
                                sy.dma_start(out=sq_st[gr % 2][:, h * 512:(h + 1) * 512],
                                             in_=q_s[h, :, gr * 512:(gr + 1) * 512]).then_inc(LD.h, 16)
                                sy.dma_start(out=sk_st[gr % 2][:, h * 512:(h + 1) * 512],
                                             in_=k_s[h, :, gr * 512:(gr + 1) * 512]).then_inc(LD.h, 16)
                            sy.dma_start(
                                out=sv_st[gr % 2][:].rearrange("p (c n) -> p c n", c=4),
                                in_=v_s2[gr * 512:(gr + 1) * 512, :].rearrange(
                                    "(c p) n -> p c n", p=128)).then_inc(LD.h, 16)
                    LD.n += 16 * 33 * GR

                    @blk.tensor
                    def _(te):
                        for t in range(CH):
                            gr, c = t // 4, t % 4
                            qo = gr % 2

                            def vsl(h):
                                return sv_st[qo][:, c * D + h * 128: c * D + h * 128 + 128]
                            if c == 0:
                                te.wait_ge(LD.h, ld0 + 16 * 33 * (gr + 1))
                            if t >= 1:
                                te.wait_ge(sV.h, dvb(t - 1) + 1)
                            for h in range(H):
                                ins = te.transpose(
                                    pT5[:, h * 128:(h + 1) * 128],
                                    sk_st[qo][:, h * 512 + c * 128: h * 512 + c * 128 + 128],
                                    sid[:])
                            sP.inc(ins)
                            if t >= 1:
                                te.wait_ge(sA.h, acb(t - 1) + 2)
                            for h in range(H):
                                ins = te.matmul(
                                    pSO[:, h * 128:(h + 1) * 128],
                                    sk_st[qo][:, h * 512 + c * 128: h * 512 + c * 128 + 128],
                                    sq_st[qo][:, h * 512 + c * 128: h * 512 + c * 128 + 128],
                                    start=True, stop=True)
                            sP.inc(ins)
                            te.wait_ge(sV.h, dvb(t) + 1)
                            if t >= 1:
                                te.wait_ge(sV.h, dvb(t - 1) + 4)
                            for h in range(8):
                                ins = te.matmul(pKV[:, h * 128:(h + 1) * 128],
                                                kdT[:, h * 128:(h + 1) * 128],
                                                vsl(h), start=True, stop=True)
                            sP.inc(ins)
                            te.wait_ge(sV.h, dvb(t) + 2)
                            if t >= 1:
                                te.wait_ge(sA.h, acb(t - 1) + 1)
                            for h in range(H):
                                te.matmul(pSO[:, h * 128:(h + 1) * 128],
                                          vsl(h),
                                          msc[:, h * 128:(h + 1) * 128],
                                          start=True, stop=False)
                                ins = te.matmul(
                                    pSO[:, h * 128:(h + 1) * 128],
                                    S_b[:, h * 128:(h + 1) * 128],
                                    qd_t[:, h * 512 + c * 128: h * 512 + c * 128 + 128],
                                    start=False, stop=True)
                            sP.inc(ins)
                            te.wait_ge(sV.h, dvb(t) + 3)
                            for h in range(8, 16):
                                ins = te.matmul(pKV[:, (h - 8) * 128:(h - 7) * 128],
                                                kdT[:, h * 128:(h + 1) * 128],
                                                vsl(h), start=True, stop=True)
                            sP.inc(ins)

                    @blk.vector
                    def _(ve):
                        for t in range(CH):
                            gr, c = t // 4, t % 4
                            if c == 0:
                                ve.wait_ge(LD.h, ld0 + 16 * 33 * (gr + 1))
                                if gr >= 1:
                                    ve.wait_ge(sP.h, peb(4 * gr))
                                ins = ve.tensor_tensor(
                                    qd_t[:].rearrange("p (h o i) -> p h o i", h=H, o=4),
                                    sq_st[gr % 2][:].rearrange("p (h o i) -> p h o i", h=H, o=4),
                                    qd_v, ALU.mult)
                                sV.inc(ins)
                            ve.wait_ge(sP.h, peb(t) + 1)
                            if t >= 1:
                                ve.wait_ge(sP.h, peb(t - 1) + 5)
                            ins = ve.tensor_tensor(
                                kdT[:].rearrange("p (h i) -> p h i", h=H),
                                pT5[:].rearrange("p (h i) -> p h i", h=H),
                                kd_v, ALU.mult)
                            sV.inc(ins)
                            ve.wait_ge(sP.h, peb(t) + 2)
                            if t >= 1:
                                ve.wait_ge(sP.h, peb(t - 1) + 4)
                            ins = ve.tensor_mul(msc[:], pSO[:], mask_a[:])
                            sV.inc(ins)
                            ve.wait_ge(sP.h, peb(t) + 3)
                            if t >= 1:
                                ve.wait_ge(sA.h, acb(t - 1) + 1)
                            ve.tensor_tensor(
                                S_f[:, 0:1024].rearrange("p (h i) -> p h i", h=8),
                                S_f[:, 0:1024].rearrange("p (h i) -> p h i", h=8),
                                bd_v[:, 0:8], ALU.mult)
                            ins = ve.tensor_add(S_f[:, 0:1024], S_f[:, 0:1024], pKV[:])
                            sV.inc(ins)
                            ve.wait_ge(sP.h, peb(t) + 5)
                            ve.tensor_tensor(
                                S_f[:, 1024:2048].rearrange("p (h i) -> p h i", h=8),
                                S_f[:, 1024:2048].rearrange("p (h i) -> p h i", h=8),
                                bd_v[:, 8:16], ALU.mult)
                            ins = ve.tensor_add(S_f[:, 1024:2048], S_f[:, 1024:2048], pKV[:])
                            sV.inc(ins)

                    @blk.scalar
                    def _(sc):
                        for t in range(CH):
                            gr, c = t // 4, t % 4
                            sc.wait_ge(sV.h, dvb(t) + 4)
                            sc.wait_ge(sP.h, peb(t) + 4)
                            ins = sc.activation(out=S_b[:], in_=S_f[:], func=AF.Copy)
                            sA.inc(ins)
                            if t >= 8:
                                sc.wait_ge(ST.h, st0 + 16 * 16 * (gr - 1))
                            ins = sc.activation(
                                out=o_st[gr % 2][:].rearrange("p (h i) -> p h i", h=H)[
                                    :, :, c * 128:(c + 1) * 128],
                                in_=pSO[:].rearrange("p (h i) -> p h i", h=H),
                                func=AF.Copy)
                            sA.inc(ins)

                    @blk.gpsimd
                    def _(gp):
                        for gr in range(GR):
                            gp.wait_ge(sA.h, acb(gr * 4 + 3) + 2)
                            for h in range(H):
                                gp.dma_start(out=a_s[h, :, gr * 512:(gr + 1) * 512],
                                             in_=o_st[gr % 2][:, h * 512:(h + 1) * 512]
                                             ).then_inc(ST.h, 16)
                    ST.n += 16 * 16 * GR

                stc = ST.n
                with nc.Block("P5c") as blk:
                    @blk.gpsimd
                    def _(gp):
                        gp.wait_ge(sV.h, sV.n)
                        gp.dma_start(out=cc_in.rearrange("(h d) e -> d h e", d=128),
                                     in_=S_f[:].rearrange("p (h e) -> p h e", h=H)
                                     ).then_inc(ST.h, 16)
                        gp.wait_ge(ST.h, stc + 16)
                        gp.collective_compute(
                            "AllGather", ALU.bypass,
                            replica_groups=[list(range(N_CORES))],
                            ins=[cc_in.opt()], outs=[cc_out.opt()],
                        ).then_inc(CC.h, 1)
                ST.n += 16
                CC.n += 1

        if "P5" in phases:
            fence("f5", cc=True)

        if "P4" in phases:
            NQ = 4
            KG = 11
            with ExitStack() as ps:
                acc = [ps.enter_context(nc.sbuf_tensor(f"p4a{i}", [128, NT], F32)) for i in range(4)]
                gu_st = [[ps.enter_context(nc.sbuf_tensor(f"p4g{j}_{i}", [128, NT], BF16))
                          for i in range(4)] for j in range(2)]
                wd_b = [ps.enter_context(nc.sbuf_tensor(f"p4wb{i}", [128, 2048], BF16)) for i in range(2)]
                gsil = ps.enter_context(nc.sbuf_tensor("p4gs", [128, 4 * NT], BF16))
                gtm_t = [ps.enter_context(nc.sbuf_tensor(f"p4t{i}", [128, 2048], BF16)) for i in range(2)]
                pD = [ps.enter_context(nc.psum_tensor(f"p4p{i}", [128, 512], F32)) for i in range(2)]
                pTr = [ps.enter_context(nc.psum_tensor(f"p4q{i}", [128, 512], BF16)) for i in range(4)]

                ld0, st0, a0, b0, p0, q0, v0, w0 = LD.n, ST.n, sA.n, sB.n, sP.n, sQ.n, sV.n, sW.n
                g4 = sG.n
                def mmi(dq, kg, dc, g):
                    return p0 + ((dq * KG + kg) * 4 + dc) * 8 + g + 1

                def addi(dq, kg, dc, g):
                    return v0 + ((dq * KG + kg) * 4 + dc) * 8 + g + 1

                gsv = gsil[:].rearrange("p (dc n) -> p dc n", dc=4)

                with nc.Block("P4") as blk:
                    @blk.sync
                    def _(sy):
                        for dq in range(NQ):
                            for kg in range(KG):
                                u = dq * KG + kg
                                if u >= 2:
                                    dqp, kgp = divmod(u - 2, KG)
                                    sy.wait_ge(sP.h, mmi(dqp, kgp, 3, 7))
                                for i in range(4):
                                    sy.dma_start(out=gu_st[u % 2][i][:],
                                                 in_=gu_s[kg * 4 + i, :, :]).then_inc(LD.h, 16)
                    LD.n += 16 * 4 * NQ * KG

                    @blk.scalar
                    def _(sc):
                        def gevac(dqq):
                            for tile in range(TT):
                                k6 = dqq * TT + tile
                                sc.wait_ge(sQ.h, q0 + k6 + 1)
                                if k6 >= 8:
                                    sc.wait_ge(ST.h, st0 + 16 * (k6 // 4 - 1))
                                ins = sc.activation(
                                    out=gtm_t[(tile // 4) % 2][:, (tile % 4) * 512:
                                                               (tile % 4) * 512 + 512],
                                    in_=pTr[tile % 4][:], func=AF.Copy)
                                sW.inc(ins)
                                if tile % 4 == 3:
                                    sc.wait_ge(sW.h, sW.n)
                                    sc.dma_start(
                                        out=g_tm[(tile - 3) * 128:(tile + 1) * 128,
                                                 dqq * 512:(dqq + 1) * 512].rearrange(
                                                     "(c p) n -> p c n", p=128),
                                        in_=gtm_t[(tile // 4) % 2][:].rearrange(
                                            "p (c n) -> p c n", c=4)).then_inc(ST.h, 16)
                        for dq in range(NQ):
                            for kg in range(KG):
                                u = dq * KG + kg
                                sc.wait_ge(sG.h, g4 + 16 * (u + 1))
                                ins = sc.activation(out=wd_b[u % 2][:], in_=wd_b[u % 2][:],
                                                    func=AF.Copy, bias=-128.0)
                                sA.inc(ins)
                            if dq >= 1:
                                gevac(dq - 1)
                            sc.wait_ge(sV.h, addi(dq, KG - 1, 3, 7))
                            if dq >= 1:
                                sc.wait_ge(sQ.h, q0 + dq * TT)
                            for dc in range(4):
                                ins = sc.activation(out=gsv[:, dc, :], in_=acc[dc][:],
                                                    func=AF.Silu,
                                                    scale=ssd[:, dq * 4 + dc:dq * 4 + dc + 1])
                            sB.inc(ins)
                        gevac(NQ - 1)

                    @blk.tensor
                    def _(te):
                        def trans_quarter(dq):
                            te.wait_ge(sB.h, b0 + dq + 1)
                            for tile in range(TT):
                                if dq * TT + tile >= 4:
                                    te.wait_ge(sW.h, w0 + dq * TT + tile - 3)
                                for dc in range(4):
                                    ins = te.transpose(
                                        pTr[tile % 4][:, dc * 128:(dc + 1) * 128],
                                        gsv[:, dc, tile * 128:(tile + 1) * 128],
                                        sid[:])
                                sQ.inc(ins)
                        for dq in range(NQ):
                            for kg in range(KG):
                                u = dq * KG + kg
                                te.wait_ge(sA.h, a0 + u + 1)
                                te.wait_ge(LD.h, ld0 + 16 * 4 * (u + 1))
                                for dc in range(4):
                                    for g in range(8):
                                        j = (dc * 8 + g) % 2
                                        if u > 0 or dc * 8 + g >= 2:
                                            te.wait_ge(sV.h, addi(dq, kg, dc, g) - 2)
                                        for i in range(4):
                                            ins = te.matmul(
                                                pD[j][:],
                                                wd_b[u % 2][:, i * 512 + dc * 128:
                                                             i * 512 + dc * 128 + 128],
                                                gu_st[u % 2][i][:, g * 512:(g + 1) * 512],
                                                start=(i == 0), stop=(i == 3))
                                        sP.inc(ins)
                            if dq >= 1:
                                trans_quarter(dq - 1)
                        trans_quarter(NQ - 1)

                    @blk.vector
                    def _(ve):
                        for dq in range(NQ):
                            for kg in range(KG):
                                for dc in range(4):
                                    for g in range(8):
                                        j = (dc * 8 + g) % 2
                                        ve.wait_ge(sP.h, mmi(dq, kg, dc, g))
                                        if kg == 0:
                                            if dq >= 1:
                                                ve.wait_ge(sB.h, b0 + dq)
                                            ins = ve.tensor_copy(
                                                acc[dc][:, g * 512:(g + 1) * 512], pD[j][:])
                                        else:
                                            ins = ve.tensor_add(
                                                acc[dc][:, g * 512:(g + 1) * 512],
                                                acc[dc][:, g * 512:(g + 1) * 512], pD[j][:])
                                        sV.inc(ins)


                    @blk.gpsimd
                    def _(gp):
                        for dq in range(NQ):
                            for kg in range(KG):
                                u = dq * KG + kg
                                if u >= 2:
                                    dqp, kgp = divmod(u - 2, KG)
                                    gp.wait_ge(sP.h, mmi(dqp, kgp, 3, 7))
                                gp.dma_start(
                                    out=wd_b[u % 2][:].rearrange("p (a n) -> p a n", a=4),
                                    in_=wdT.rearrange("(a p) n -> p a n", p=128)[
                                        :, kg * 4:kg * 4 + 4, dq * 512:(dq + 1) * 512],
                                ).then_inc(sG.h, 16)
                    sG.n += 16 * NQ * KG
                    ST.n += 16 * NQ * (TT // 4)

        if "P4" in phases:
            fence("f4")

        if "P55" in phases and "P6" in phases:
            with ExitStack() as ps:
                qde = ps.enter_context(nc.sbuf_tensor("p7qe", [128, H * 512], BF16))
                bd5 = ps.enter_context(nc.sbuf_tensor("p7bd", [128, H], F32))
                pS6 = [ps.enter_context(nc.psum_tensor(f"p7p{i}", [128, D], F32)) for i in range(2)]

                with ExitStack() as ps2:
                    dneg = ps2.enter_context(nc.sbuf_tensor("p7i", [128, 513], F32))
                    ld_r = ps2.enter_context(nc.sbuf_tensor("p7ld", [128, H * 128], F32))

                    g0, a00, v00 = sG.n, sA.n, sV.n
                    with nc.Block("P7s") as blk:
                        @blk.gpsimd
                        def _(gp):
                            gp.iota(dneg[:, 0:512], [[-1, 512]], base=0, channel_multiplier=0,
                                    allow_small_or_imprecise_dtypes=True)
                            ins = gp.memset(dneg[:, 512:513], -512.0)
                            sG.inc(ins)

                        @blk.scalar
                        def _(sc):
                            sc.wait_ge(sG.h, g0 + 1)
                            for h in range(H):
                                sc.activation(out=qde[:, h * 512:(h + 1) * 512],
                                              in_=dneg[:, 0:512],
                                              func=AF.Exp, scale=sslope[:, h:h + 1])
                                ins = sc.activation(out=bd5[:, h:h + 1], in_=dneg[:, 512:513],
                                                    func=AF.Exp, scale=sslope[:, h:h + 1])
                            sA.inc(ins)

                        @blk.vector
                        def _(ve):
                            ve.memset(S_f[:], 0.0)

                    ld0 = LD.n
                    v0 = sV.n
                    bd5_v = bd5[:].rearrange("p (h o) -> p h o", o=1).broadcast_to([128, H, 128])
                    with nc.Block("P7a") as blk:
                        @blk.sync
                        def _(sy):
                            sy.wait_ge(CC.h, CC.n)
                            for r in range(N_CORES):
                                if r >= 1:
                                    sy.wait_ge(sV.h, v0 + r)
                                sy.dma_start(
                                    out=ld_r[:].rearrange("p (h e) -> p h e", h=H),
                                    in_=cc_out.rearrange("(r h d) e -> r d h e",
                                                         r=N_CORES, d=128)[r],
                                ).then_inc(LD.h, 16)
                        LD.n += 16 * N_CORES

                        @blk.vector
                        def _(ve):
                            for r in range(N_CORES):
                                ve.wait_ge(LD.h, ld0 + 16 * (r + 1))
                                ins = ve.scalar_tensor_tensor(S_f[:], ld_r[:], ssel[:, r:r + 1],
                                                              S_f[:], ALU.mult, ALU.add)
                                sV.inc(ins)

                        @blk.scalar
                        def _(sc):
                            sc.wait_ge(sV.h, v0 + N_CORES)
                            ins = sc.activation(out=S_b[:], in_=S_f[:], func=AF.Copy)
                            sA.inc(ins)

                    ld0, st0, a0, p0, q0, v0, w0, d0, g6, c0 = (
                        LD.n, ST.n, sA.n, sP.n, sQ.n, sV.n, sW.n, sD.n, sG.n, sC.n)

                wo_b = ps.enter_context(nc.sbuf_tensor("p7wb", [128, DC * D], BF16))
                q5 = [ps.enter_context(nc.sbuf_tensor(f"p7q{i}", [128, H * 512], BF16)) for i in range(2)]
                at5 = [ps.enter_context(nc.sbuf_tensor(f"p7a{i}", [128, H * 512], BF16)) for i in range(2)]
                g_f = [ps.enter_context(nc.sbuf_tensor(f"p7gf{i}", [128, D], F32)) for i in range(2)]
                sgp = ps.enter_context(nc.sbuf_tensor("p7s", [128, D], F32))
                x_st6 = [ps.enter_context(nc.sbuf_tensor(f"p7x{i}", [128, D], F32)) for i in range(2)]

                with nc.Block("P7m") as blk:
                    @blk.sync
                    def _(sy):
                        for g in range(GR):
                            if g >= 2:
                                sy.wait_ge(sP.h, p0 + 4 * (g - 1))
                                sy.wait_ge(sQ.h, q0 + 4 * (g - 1))
                            for h in range(H):
                                sy.dma_start(out=at5[g % 2][:, h * 512:(h + 1) * 512],
                                             in_=a_s[h, :, g * 512:(g + 1) * 512]
                                             ).then_inc(LD.h, 16)
                                sy.dma_start(out=q5[g % 2][:, h * 512:(h + 1) * 512],
                                             in_=q_s[h, :, g * 512:(g + 1) * 512]
                                             ).then_inc(LD.h, 16)
                            for tl in range(4):
                                t = g * 4 + tl
                                if t >= 2:
                                    sy.wait_ge(ST.h, st0 + 16 * (t - 1))
                                sy.dma_start(out=x_st6[t % 2][:],
                                             in_=x_in[t * 128:(t + 1) * 128, :]
                                             ).then_inc(LD.h, 16)
                    LD.n += 16 * 36 * GR

                    @blk.gpsimd
                    def _(gp):
                        gp.dma_start(
                            out=wo_b[:].rearrange("p (c n) -> p c n", c=DC),
                            in_=wout.rearrange("(c p) n -> p c n", p=128),
                        ).then_inc(sG.h, 16)
                        for t in range(TT):
                            if t >= 2:
                                gp.wait_ge(sV.h, v0 + t - 1)
                            gp.dma_start(out=g_f[t % 2][:],
                                         in_=g_tm[t * 128:(t + 1) * 128, :]).then_inc(sG.h, 16)
                    sG.n += 16 * (TT + 1)

                    @blk.vector
                    def _(ve):
                        nocorr = "NOCORR" in phases
                        for g in range(GR):
                            ve.wait_ge(LD.h, ld0 + 16 * (36 * g + 32))
                            ins = ve.tensor_mul(q5[g % 2][:], q5[g % 2][:], qde[:])
                            sD.inc(ins)
                            for qq in range(4):
                                if nocorr:
                                    sW.inc(ve.nop())
                                    continue
                                ve.wait_ge(sQ.h, q0 + g * 4 + qq + 1)
                                ins = ve.tensor_add(at5[g % 2][:, qq * 2048:(qq + 1) * 2048],
                                                    pS6[qq % 2][:],
                                                    at5[g % 2][:, qq * 2048:(qq + 1) * 2048])
                                sW.inc(ins)
                            ve.wait_ge(sA.h, a0 + g)
                            ins = ve.tensor_tensor(
                                S_f[:].rearrange("p (h i) -> p h i", h=H),
                                S_f[:].rearrange("p (h i) -> p h i", h=H),
                                bd5_v, ALU.mult)
                            sD.inc(ins)
                            for tl in range(4):
                                t = g * 4 + tl
                                ve.wait_ge(sP.h, p0 + t + 1)
                                ve.wait_ge(sG.h, g6 + 16 * (t + 2))
                                ins = ve.tensor_mul(sgp[:], pS6[t % 2][:], g_f[t % 2][:])
                                sV.inc(ins)
                                ve.wait_ge(LD.h, ld0 + 16 * (36 * g + 32 + tl + 1))
                                ins = ve.tensor_add(x_st6[t % 2][:], sgp[:], x_st6[t % 2][:])
                                sC.inc(ins)

                    @blk.scalar
                    def _(sc):
                        for g in range(GR):
                            if g >= 1:
                                sc.wait_ge(sD.h, d0 + 2 * g)
                                sc.wait_ge(sQ.h, q0 + 4 * g)
                                ins = sc.activation(out=S_b[:], in_=S_f[:], func=AF.Copy)
                                sA.inc(ins)
                            for tl in range(4):
                                t = g * 4 + tl
                                sc.wait_ge(sC.h, c0 + t + 1)
                                sc.dma_start(out=out[t * 128:(t + 1) * 128, :],
                                             in_=x_st6[t % 2][:]).then_inc(ST.h, 16)
                    ST.n += 16 * TT

                    @blk.tensor
                    def _(te):
                        te.wait_ge(sG.h, g6 + 16)
                        for g in range(GR):
                            te.wait_ge(sD.h, d0 + 2 * g + 1)
                            te.wait_ge(sA.h, a0 + g)
                            if g >= 1:
                                te.wait_ge(sV.h, v0 + 4 * g)
                            for qq in range(4):
                                if "NOCORR" in phases:
                                    sQ.inc(te.nop())
                                    continue
                                if qq >= 2:
                                    te.wait_ge(sW.h, w0 + 4 * g + qq - 1)
                                for hh in range(4):
                                    h = qq * 4 + hh
                                    ins = te.matmul(pS6[qq % 2][:, hh * 512:(hh + 1) * 512],
                                                    S_b[:, h * 128:(h + 1) * 128],
                                                    q5[g % 2][:, h * 512:(h + 1) * 512],
                                                    start=True, stop=True)
                                sQ.inc(ins)
                            te.wait_ge(sW.h, w0 + 4 * (g + 1))
                            for tl in range(4):
                                t = g * 4 + tl
                                if t >= 2:
                                    te.wait_ge(sV.h, v0 + t - 1)
                                for nn in range(4):
                                    for cc in range(DC):
                                        te.matmul(
                                            pS6[t % 2][:, nn * 512:(nn + 1) * 512],
                                            at5[g % 2][:, cc * 512 + tl * 128:
                                                         cc * 512 + tl * 128 + 128],
                                            wo_b[:, cc * D + nn * 512: cc * D + nn * 512 + 512],
                                            start=(cc == 0), stop=False)
                                    ins = te.matmul(pS6[t % 2][:, nn * 512:(nn + 1) * 512],
                                                    sones[:, 0:128],
                                                    sbout_b[:, nn * 512:(nn + 1) * 512],
                                                    start=False, stop=True)
                                sP.inc(ins)

        with nc.Block("drain") as blk:
            @blk.gpsimd
            def _(gp):
                gp.wait_ge(ST.h, ST.n)
            @blk.sync
            def _(sy):
                sy.wait_ge(LD.h, LD.n)

    return nc


def _prep_inputs(L0_attn_x, L0_attn_Wqkv, L0_attn_bqkv, L0_attn_Wout, L0_attn_bout,
                 L0_attn_slopes, L0_ffn_wq_gate, L0_ffn_s_gate, L0_ffn_wq_up,
                 L0_ffn_s_up, L0_ffn_wq_down, L0_ffn_s_down):
    x = np.ascontiguousarray(np.asarray(L0_attn_x, dtype=np.float32)).reshape(4 * 8192, D)
    shared = {
        "wqkv": np.ascontiguousarray(np.asarray(L0_attn_Wqkv, dtype=np.float32)),
        "bqkv_c": np.ascontiguousarray(np.asarray(L0_attn_bqkv, np.float32).reshape(48, 128).T),
        "wout": np.ascontiguousarray(np.asarray(L0_attn_Wout, dtype=np.float32)),
        "bout_r": np.ascontiguousarray(np.asarray(L0_attn_bout, np.float32).reshape(1, D)),
        "slope_b": np.ascontiguousarray(
            np.broadcast_to(np.asarray(L0_attn_slopes, np.float32)[None, :], (128, H))),
        "wgT": np.ascontiguousarray(np.asarray(L0_ffn_wq_gate, np.int32).T),
        "wuT": np.ascontiguousarray(np.asarray(L0_ffn_wq_up, np.int32).T),
        "wdT": np.ascontiguousarray(np.asarray(L0_ffn_wq_down, np.int32).T),
        "sg_c": np.ascontiguousarray(np.asarray(L0_ffn_s_gate, np.float32).reshape(HC, 128).T),
        "su_c": np.ascontiguousarray(np.asarray(L0_ffn_s_up, np.float32).reshape(HC, 128).T),
        "sd_c": np.ascontiguousarray(np.asarray(L0_ffn_s_down, np.float32).reshape(DC, 128).T),
        "ident": np.eye(128, dtype=ml_dtypes.bfloat16),
        "ones_b": np.ones((1, 128), dtype=ml_dtypes.bfloat16),
        "ones_f": np.ones((1, 128), dtype=np.float32),
        "bqv_row": np.ascontiguousarray(
            np.asarray(L0_attn_bqkv, np.float32)[2 * D:].reshape(1, D)),
    }
    in_maps = []
    for c in range(N_CORES):
        sel = np.zeros((128, N_CORES), dtype=np.float32)
        if c % 2 == 1:
            sel[:, c - 1] = 1.0
        m = dict(shared)
        m["x"] = x[c * NT:(c + 1) * NT]
        m["selm"] = sel
        in_maps.append(m)
    return in_maps


def kernel(**inputs):
    in_maps = _prep_inputs(**inputs)
    nc = build_program()
    res = run_bass_kernel_spmd(nc, in_maps, core_ids=list(range(N_CORES)))
    outs = [r["out"] for r in res.results]
    return np.concatenate(outs, axis=0).reshape(4, 8192, D)


# version 16 (final)
# speedup vs baseline: 1.0162x; 1.0162x over previous
import numpy as np
import ml_dtypes
import concourse.bass as bass
import concourse.mybir as mybir
from concourse.bass_utils import run_bass_kernel_spmd
from contextlib import ExitStack

F32 = mybir.dt.float32
BF16 = mybir.dt.bfloat16
I32 = mybir.dt.int32
AF = mybir.ActivationFunctionType
ALU = mybir.AluOpType
AX = mybir.AxisListType

D = 2048
DC = 16
NT = 4096
TT = 32
H = 16
HID = 5632
HC = 44
CH = 32
GR = 8
EPS = 1e-5
N_CORES = 8


class Sem:
    def __init__(self, handle):
        self.h = handle
        self.n = 0

    def inc(self, instr, k=1):
        instr.then_inc(self.h, k)
        self.n += k
        return self.n


def build_program(phases=("P1", "P2", "P3", "P5", "P4", "P55", "P6"), dbg=()):
    nc = bass.Bass("TRN2", target_bir_lowering=False)

    def scratch(name, shape, dt):
        kind = "ExternalOutput" if name in dbg else "Internal"
        return nc.dram_tensor(name, shape, dt, kind=kind).ap()

    x_in = nc.dram_tensor("x", [NT, D], F32, kind="ExternalInput").ap()
    wqkv = nc.dram_tensor("wqkv", [D, 3 * D], F32, kind="ExternalInput").ap()
    bqkv_c = nc.dram_tensor("bqkv_c", [128, 48], F32, kind="ExternalInput").ap()
    wout = nc.dram_tensor("wout", [D, D], F32, kind="ExternalInput").ap()
    bout_r = nc.dram_tensor("bout_r", [1, D], F32, kind="ExternalInput").ap()
    slope_b = nc.dram_tensor("slope_b", [128, H], F32, kind="ExternalInput").ap()
    wgT = nc.dram_tensor("wgT", [D, HID], I32, kind="ExternalInput").ap()
    wuT = nc.dram_tensor("wuT", [D, HID], I32, kind="ExternalInput").ap()
    wdT = nc.dram_tensor("wdT", [HID, D], I32, kind="ExternalInput").ap()
    sg_c = nc.dram_tensor("sg_c", [128, HC], F32, kind="ExternalInput").ap()
    su_c = nc.dram_tensor("su_c", [128, HC], F32, kind="ExternalInput").ap()
    sd_c = nc.dram_tensor("sd_c", [128, DC], F32, kind="ExternalInput").ap()
    ident = nc.dram_tensor("ident", [128, 128], BF16, kind="ExternalInput").ap()
    ones_b = nc.dram_tensor("ones_b", [1, 128], BF16, kind="ExternalInput").ap()
    ones_f = nc.dram_tensor("ones_f", [1, 128], F32, kind="ExternalInput").ap()
    bqv_row = nc.dram_tensor("bqv_row", [1, D], F32, kind="ExternalInput").ap()
    selm = nc.dram_tensor("selm", [128, N_CORES], F32, kind="ExternalInput").ap()
    out = nc.dram_tensor("out", [NT, D], F32, kind="ExternalOutput").ap()

    pad0 = nc.dram_tensor("pad0", [4096, 1024], F32).ap()
    cc_in = nc.dram_tensor("cc_in", [H * 128, 128], F32).ap()
    pad1 = nc.dram_tensor("pad1", [4096, 1024], F32).ap()
    cc_out = nc.dram_tensor("cc_out", [N_CORES * H * 128, 128], F32).ap()
    pad2 = nc.dram_tensor("pad2", [4096, 1024], F32).ap()
    q_s = scratch("q_s", [H, 128, NT], BF16)
    k_s = scratch("k_s", [H, 128, NT], BF16)
    v_s2 = scratch("v_s2", [NT, D], BF16)
    gu_s = scratch("gu_s", [HC, 128, NT], BF16)
    a_s = scratch("a_s", [H, 128, NT], BF16)
    g_tm = scratch("g_tm", [NT, D], BF16)
    s_dbg = scratch("s_dbg", [NT, D], F32)

    es = ExitStack()

    def sb(name, shape, dt):
        return es.enter_context(nc.sbuf_tensor(name, shape, dt))

    def sem(name):
        return Sem(es.enter_context(nc.semaphore(name)))

    with es:
        sid = sb("sid", [128, 128], BF16)
        sones = sb("sones", [1, 128], BF16)
        sonesf = sb("sonesf", [1, 128], F32)
        sbq = sb("sbq", [128, 48], F32)
        sbout_b = sb("sbout_b", [1, D], BF16)
        sslope = sb("sslope", [128, H], F32)
        ssg = sb("ssg", [128, HC], F32)
        ssu = sb("ssu", [128, HC], F32)
        ssd = sb("ssd", [128, DC], F32)
        ssel = sb("ssel", [128, N_CORES], F32)
        S_f = sb("S_f", [128, H * 128], F32)
        S_b = sb("S_b", [128, H * 128], BF16)

        LD = sem("LD")
        ST = sem("ST")
        CC = sem("CC")
        sA = sem("sA")
        sB = sem("sB")
        sC = sem("sC")
        sD = sem("sD")
        sV = sem("sV")
        sW = sem("sW")
        sP = sem("sP")
        sQ = sem("sQ")
        sG = sem("sG")

        def fence(name, cc=False):
            with nc.Block(name) as blk:
                @blk.gpsimd
                def _(gp):
                    gp.wait_ge(ST.h, ST.n)
                    if cc:
                        gp.wait_ge(CC.h, CC.n)
                @blk.sync
                def _(sy):
                    sy.wait_ge(LD.h, LD.n)

        with ExitStack() as ps0:
            sbout = ps0.enter_context(nc.sbuf_tensor("sbout", [1, D], F32))
            with nc.Block("setup") as blk:
                @blk.sync
                def _(sy):
                    sy.dma_start(out=sid[:], in_=ident).then_inc(LD.h, 16)
                    sy.dma_start(out=sones[:], in_=ones_b).then_inc(LD.h, 16)
                    sy.dma_start(out=sonesf[:], in_=ones_f).then_inc(LD.h, 16)
                    sy.dma_start(out=sbq[:], in_=bqkv_c).then_inc(LD.h, 16)
                    sy.dma_start(out=sbout[:], in_=bout_r).then_inc(LD.h, 16)
                    sy.dma_start(out=sslope[:], in_=slope_b).then_inc(LD.h, 16)
                    sy.dma_start(out=ssg[:], in_=sg_c).then_inc(LD.h, 16)
                    sy.dma_start(out=ssu[:], in_=su_c).then_inc(LD.h, 16)
                    sy.dma_start(out=ssd[:], in_=sd_c).then_inc(LD.h, 16)
                    sy.dma_start(out=ssel[:], in_=selm).then_inc(LD.h, 16)
                LD.n += 16 * 10

                @blk.scalar
                def _(sc):
                    sc.wait_ge(LD.h, LD.n)
                    ins = sc.activation(out=sbout_b[:], in_=sbout[:], func=AF.Copy)
                    sA.inc(ins)

        es_xn = ExitStack()
        xn_fm = es_xn.enter_context(nc.sbuf_tensor("xn_fm", [128, DC * NT], BF16))

        if "P1" in phases:
            with ExitStack() as ps:
                x_st = [ps.enter_context(nc.sbuf_tensor(f"p1x{i}", [128, D], F32)) for i in range(2)]
                xsq = ps.enter_context(nc.sbuf_tensor("p1sq", [128, D], F32))
                xn_tm = [ps.enter_context(nc.sbuf_tensor(f"p1n{i}", [128, D], BF16)) for i in range(2)]
                sums = ps.enter_context(nc.sbuf_tensor("p1sums", [128, TT], F32))
                ssqs = ps.enter_context(nc.sbuf_tensor("p1ssqs", [128, TT], F32))
                mean = ps.enter_context(nc.sbuf_tensor("p1mean", [128, TT], F32))
                e2 = ps.enter_context(nc.sbuf_tensor("p1e2", [128, TT], F32))
                msq = ps.enter_context(nc.sbuf_tensor("p1msq", [128, TT], F32))
                rstd = ps.enter_context(nc.sbuf_tensor("p1rs", [128, TT], F32))
                nmr = ps.enter_context(nc.sbuf_tensor("p1nm", [128, TT], F32))
                pT = [ps.enter_context(nc.psum_tensor(f"p1p{i}", [128, D], BF16)) for i in range(2)]

                ld0 = LD.n
                a0, v0, w0, d0, c0, b0, p0, q0 = sA.n, sV.n, sW.n, sD.n, sC.n, sB.n, sP.n, sQ.n
                with nc.Block("P1") as blk:
                    @blk.sync
                    def _(sy):
                        for t in range(TT):
                            if t >= 2:
                                sy.wait_ge(sB.h, b0 + t - 1)
                            sy.dma_start(out=x_st[t % 2][:],
                                         in_=x_in[t * 128:(t + 1) * 128, :]).then_inc(LD.h, 16)
                    LD.n += 16 * TT

                    @blk.scalar
                    def _(sc):
                        for t in range(TT):
                            sc.wait_ge(LD.h, ld0 + 16 * (t + 1))
                            ins = sc.activation(out=xsq[:], in_=x_st[t % 2][:], func=AF.Square,
                                                accum_out=ssqs[:, t:t + 1])
                            sA.inc(ins)
                            sc.wait_ge(sW.h, w0 + t + 1)
                            sc.activation(out=e2[:, t:t + 1], in_=e2[:, t:t + 1],
                                          func=AF.Ln)
                            ins = sc.activation(out=rstd[:, t:t + 1], in_=e2[:, t:t + 1],
                                                func=AF.Exp, scale=-0.5)
                            sD.inc(ins)
                            sc.wait_ge(sC.h, c0 + t + 1)
                            if t >= 2:
                                sc.wait_ge(sP.h, p0 + t - 1)
                            ins = sc.activation(out=xn_tm[t % 2][:], in_=x_st[t % 2][:],
                                                func=AF.Identity, scale=rstd[:, t:t + 1],
                                                bias=nmr[:, t:t + 1])
                            sB.inc(ins)

                    @blk.vector
                    def _(ve):
                        for t in range(TT):
                            ve.wait_ge(LD.h, ld0 + 16 * (t + 1))
                            ins = ve.reduce_sum(sums[:, t:t + 1], x_st[t % 2][:], axis=AX.X)
                            sV.inc(ins)
                            ve.wait_ge(sA.h, a0 + t + 1)
                            ve.tensor_scalar_mul(mean[:, t:t + 1], sums[:, t:t + 1], 1.0 / D)
                            ve.tensor_scalar_mul(e2[:, t:t + 1], ssqs[:, t:t + 1], 1.0 / D)
                            ve.tensor_mul(msq[:, t:t + 1], mean[:, t:t + 1], mean[:, t:t + 1])
                            ve.tensor_sub(e2[:, t:t + 1], e2[:, t:t + 1], msq[:, t:t + 1])
                            ins = ve.tensor_scalar_add(e2[:, t:t + 1], e2[:, t:t + 1], EPS)
                            sW.inc(ins)
                            ve.wait_ge(sD.h, d0 + t + 1)
                            ins = ve.scalar_tensor_tensor(nmr[:, t:t + 1], mean[:, t:t + 1],
                                                          -1.0, rstd[:, t:t + 1],
                                                          ALU.mult, ALU.mult)
                            sC.inc(ins)
                            if t >= 1:
                                ve.wait_ge(sP.h, p0 + t)
                                ins = ve.tensor_copy(
                                    xn_fm[:].rearrange("p (c n) -> p c n", c=DC)[
                                        :, :, (t - 1) * 128:t * 128],
                                    pT[(t - 1) % 2][:].rearrange("p (c i) -> p c i", c=DC))
                                sQ.inc(ins)
                        ve.wait_ge(sP.h, p0 + TT)
                        ins = ve.tensor_copy(
                            xn_fm[:].rearrange("p (c n) -> p c n", c=DC)[
                                :, :, (TT - 1) * 128:TT * 128],
                            pT[(TT - 1) % 2][:].rearrange("p (c i) -> p c i", c=DC))
                        sQ.inc(ins)

                    @blk.tensor
                    def _(te):
                        for t in range(TT):
                            te.wait_ge(sB.h, b0 + t + 1)
                            if t >= 2:
                                te.wait_ge(sQ.h, q0 + t - 1)
                            for d in range(DC):
                                ins = te.transpose(pT[t % 2][:, d * 128:(d + 1) * 128],
                                                   xn_tm[t % 2][:, d * 128:(d + 1) * 128],
                                                   sid[:])
                            sP.inc(ins)

        if "P1" in phases:
            fence("f1")

        if "P2" in phases:
            with ExitStack() as ps:
                wbf = [ps.enter_context(nc.sbuf_tensor(f"p2b{i}", [128, D], BF16)) for i in range(2)]
                ev = [ps.enter_context(nc.sbuf_tensor(f"p2e{i}", [128, 512], BF16)) for i in range(4)]
                pQ = [ps.enter_context(nc.psum_tensor(f"p2p{i}", [128, 512], F32)) for i in range(4)]

                ld0, st0, a0, b0, p0 = LD.n, ST.n, sA.n, sB.n, sP.n
                with nc.Block("P2a") as blk:
                    @blk.gpsimd
                    def _(gp):
                        for cg in range(32):
                            if cg >= 2:
                                gp.wait_ge(sP.h, p0 + 8 * (cg - 1))
                            gp.dma_start(
                                out=wbf[cg % 2][:].rearrange("p (c m) -> p c m", c=DC),
                                in_=wqkv[:, cg * 128:(cg + 1) * 128].rearrange(
                                    "(c p) m -> p c m", p=128),
                            ).then_inc(LD.h, 16)
                    LD.n += 16 * 32

                    @blk.scalar
                    def _(sc):
                        for cg in range(32):
                            for g in range(8):
                                k = cg * 8 + g
                                sc.wait_ge(sP.h, p0 + k + 1)
                                if k >= 4:
                                    sc.wait_ge(ST.h, st0 + 16 * (k - 3))
                                ins = sc.activation(out=ev[k % 4][:], in_=pQ[k % 4][:],
                                                    func=AF.Identity, bias=sbq[:, cg:cg + 1])
                                sB.inc(ins)

                    @blk.tensor
                    def _(te):
                        for cg in range(32):
                            te.wait_ge(LD.h, ld0 + 16 * (cg + 1))
                            for g in range(8):
                                k = cg * 8 + g
                                if k >= 4:
                                    te.wait_ge(sB.h, b0 + k - 3)
                                for d in range(DC):
                                    ins = te.matmul(
                                        pQ[k % 4][:],
                                        wbf[cg % 2][:, d * 128:(d + 1) * 128],
                                        xn_fm[:, d * NT + g * 512: d * NT + g * 512 + 512],
                                        start=(d == 0), stop=(d == DC - 1))
                                sP.inc(ins)

                    @blk.sync
                    def _(sy):
                        for cg in range(32):
                            typ, h = divmod(cg, H)
                            dst = (q_s, k_s)[typ]
                            for g in range(8):
                                k = cg * 8 + g
                                sy.wait_ge(sB.h, b0 + k + 1)
                                sy.dma_start(out=dst[h, :, g * 512:(g + 1) * 512],
                                             in_=ev[k % 4][:]).then_inc(ST.h, 16)
                    ST.n += 16 * 8 * 32

        if "P2" in phases:
            fence("f2a")

        if "P2" in phases:
            with ExitStack() as ps:
                wv_bf = [ps.enter_context(nc.sbuf_tensor(f"p2vb{i}", [128, 16 * 512], BF16)) for i in range(2)]
                bqv_r = ps.enter_context(nc.sbuf_tensor("p2bqr", [1, D], F32))
                bqv = ps.enter_context(nc.sbuf_tensor("p2bqv", [128, D], F32))
                ev2 = [ps.enter_context(nc.sbuf_tensor(f"p2e2{i}", [128, 512], BF16)) for i in range(4)]
                pQ2 = [ps.enter_context(nc.psum_tensor(f"p2q2{i}", [128, 512], F32)) for i in range(4)]

                ld0, st0, a0, p0, q0, v0, w0 = LD.n, ST.n, sA.n, sP.n, sQ.n, sV.n, sW.n
                with nc.Block("P2b") as blk:
                    @blk.sync
                    def _(sy):
                        sy.dma_start(out=bqv_r[:], in_=bqv_row).then_inc(LD.h, 16)
                        for vg in range(4):
                            for tt in range(TT):
                                k = vg * TT + tt
                                sy.wait_ge(sW.h, w0 + k + 1)
                                sy.dma_start(
                                    out=v_s2[tt * 128:(tt + 1) * 128,
                                             vg * 512:(vg + 1) * 512],
                                    in_=ev2[k % 4][:]).then_inc(ST.h, 16)
                    ST.n += 16 * 128

                    @blk.gpsimd
                    def _(gp):
                        for vg in range(4):
                            if vg >= 2:
                                gp.wait_ge(sP.h, p0 + TT * (vg - 1))
                            ins = gp.dma_start(
                                out=wv_bf[vg % 2][:].rearrange("p (c n) -> p c n", c=DC),
                                in_=wqkv[:, 2 * D + vg * 512: 2 * D + (vg + 1) * 512
                                         ].rearrange("(c p) n -> p c n", p=128))
                            ins.then_inc(LD.h, 16)
                    LD.n += 16 * 5

                    @blk.tensor
                    def _(te):
                        te.wait_ge(LD.h, ld0 + 16)
                        for i in range(4):
                            ins = te.matmul(pQ2[i][:], sonesf[:, 0:128],
                                            bqv_r[:, i * 512:(i + 1) * 512],
                                            start=True, stop=True)
                        sQ.inc(ins)
                        te.wait_ge(sV.h, v0 + 1)
                        for vg in range(4):
                            te.wait_ge(LD.h, ld0 + 16 * (vg + 2))
                            for tt in range(TT):
                                k = vg * TT + tt
                                if k >= 4:
                                    te.wait_ge(sW.h, w0 + k - 3)
                                for d in range(DC):
                                    ins = te.matmul(
                                        pQ2[k % 4][:],
                                        xn_fm[:, d * NT + tt * 128: d * NT + tt * 128 + 128],
                                        wv_bf[vg % 2][:, d * 512:(d + 1) * 512],
                                        start=(d == 0), stop=(d == DC - 1))
                                sP.inc(ins)

                    @blk.vector
                    def _(ve):
                        ve.wait_ge(sQ.h, q0 + 1)
                        for i in range(4):
                            ve.tensor_copy(bqv[:, i * 512:(i + 1) * 512], pQ2[i][:])
                        sV.inc(ve.nop())
                        for vg in range(4):
                            for tt in range(TT):
                                k = vg * TT + tt
                                ve.wait_ge(sP.h, p0 + k + 1)
                                if k >= 4:
                                    ve.wait_ge(ST.h, st0 + 16 * (k - 3))
                                ins = ve.tensor_add(ev2[k % 4][:], pQ2[k % 4][:],
                                                    bqv[:, vg * 512:(vg + 1) * 512])
                                sW.inc(ins)

        if "P2" in phases:
            fence("f2")

        if "P3" in phases:
            with ExitStack() as ps:
                wbfs = [ps.enter_context(nc.sbuf_tensor(f"p3b{i}", [128, D], BF16)) for i in range(4)]
                silu_t = [ps.enter_context(nc.sbuf_tensor(f"p3s{i}", [128, 512], F32)) for i in range(2)]
                up_t = [ps.enter_context(nc.sbuf_tensor(f"p3u{i}", [128, 512], F32)) for i in range(2)]
                gu_t = [ps.enter_context(nc.sbuf_tensor(f"p3g{i}", [128, 512], BF16)) for i in range(4)]
                pG = [ps.enter_context(nc.psum_tensor(f"p3pg{i}", [128, 512], F32)) for i in range(2)]
                pU = [ps.enter_context(nc.psum_tensor(f"p3pu{i}", [128, 512], F32)) for i in range(2)]

                ld0, st0, a0, b0, p0, v0, w0 = LD.n, ST.n, sA.n, sB.n, sP.n, sV.n, sW.n
                with nc.Block("P3") as blk:
                    @blk.gpsimd
                    def _(gp):
                        for hc in range(HC):
                            if hc >= 2:
                                gp.wait_ge(sP.h, p0 + 8 * (hc - 1))
                            gp.dma_start(
                                out=wbfs[(2 * hc) % 4][:].rearrange("p (c h) -> p c h", c=DC),
                                in_=wgT[:, hc * 128:(hc + 1) * 128].rearrange(
                                    "(c p) h -> p c h", p=128)).then_inc(LD.h, 16)
                            gp.dma_start(
                                out=wbfs[(2 * hc + 1) % 4][:].rearrange("p (c h) -> p c h", c=DC),
                                in_=wuT[:, hc * 128:(hc + 1) * 128].rearrange(
                                    "(c p) h -> p c h", p=128)).then_inc(LD.h, 16)
                    LD.n += 32 * HC

                    @blk.scalar
                    def _(sc):
                        def silus(hc):
                            for g in range(8):
                                k = hc * 8 + g
                                sc.wait_ge(sP.h, p0 + k + 1)
                                if k >= 2:
                                    sc.wait_ge(sW.h, w0 + k - 1)
                                ins = sc.activation(out=silu_t[k % 2][:], in_=pG[k % 2][:],
                                                    func=AF.Silu, scale=ssg[:, hc:hc + 1])
                                sB.inc(ins)
                        for hc in range(HC):
                            sc.wait_ge(LD.h, ld0 + 16 * (2 * hc + 2))
                            ins = sc.activation(out=wbfs[(2 * hc) % 4][:],
                                                in_=wbfs[(2 * hc) % 4][:],
                                                func=AF.Copy, bias=-128.0)
                            sA.inc(ins)
                            ins = sc.activation(out=wbfs[(2 * hc + 1) % 4][:],
                                                in_=wbfs[(2 * hc + 1) % 4][:],
                                                func=AF.Copy, bias=-128.0)
                            sA.inc(ins)
                            if hc >= 1:
                                silus(hc - 1)
                        silus(HC - 1)

                    @blk.tensor
                    def _(te):
                        for hc in range(HC):
                            te.wait_ge(sA.h, a0 + 2 * hc + 2)
                            for g in range(8):
                                k = hc * 8 + g
                                if k >= 2:
                                    te.wait_ge(sB.h, b0 + k - 1)
                                    te.wait_ge(sV.h, v0 + k - 1)
                                for d in range(DC):
                                    te.matmul(pG[k % 2][:],
                                              wbfs[(2 * hc) % 4][:, d * 128:(d + 1) * 128],
                                              xn_fm[:, d * NT + g * 512: d * NT + g * 512 + 512],
                                              start=(d == 0), stop=(d == DC - 1))
                                for d in range(DC):
                                    ins = te.matmul(
                                        pU[k % 2][:],
                                        wbfs[(2 * hc + 1) % 4][:, d * 128:(d + 1) * 128],
                                        xn_fm[:, d * NT + g * 512: d * NT + g * 512 + 512],
                                        start=(d == 0), stop=(d == DC - 1))
                                sP.inc(ins)

                    @blk.vector
                    def _(ve):
                        for hc in range(HC):
                            for g in range(8):
                                k = hc * 8 + g
                                ve.wait_ge(sP.h, p0 + k + 1)
                                if k >= 2:
                                    ve.wait_ge(sW.h, w0 + k - 1)
                                ins = ve.tensor_scalar(up_t[k % 2][:], pU[k % 2][:],
                                                       ssu[:, hc:hc + 1], None, ALU.mult)
                                sV.inc(ins)
                                ve.wait_ge(sB.h, b0 + k + 1)
                                if k >= 4:
                                    ve.wait_ge(ST.h, st0 + 16 * (k - 3))
                                ins = ve.tensor_mul(gu_t[k % 4][:], silu_t[k % 2][:],
                                                    up_t[k % 2][:])
                                sW.inc(ins)

                    @blk.sync
                    def _(sy):
                        for hc in range(HC):
                            for g in range(8):
                                k = hc * 8 + g
                                sy.wait_ge(sW.h, w0 + k + 1)
                                sy.dma_start(out=gu_s[hc, :, g * 512:(g + 1) * 512],
                                             in_=gu_t[k % 4][:]).then_inc(ST.h, 16)
                    ST.n += 16 * 8 * HC

        if "P3" in phases:
            fence("f3")

        es_xn.close()

        if "P5" in phases:
            with ExitStack() as ps:
                mask_a = ps.enter_context(nc.sbuf_tensor("mask_a", [128, H * 128], BF16))
                qdec_s = ps.enter_context(nc.sbuf_tensor("qdec_s", [128, H * 128], F32))
                kdec_s = ps.enter_context(nc.sbuf_tensor("kdec_s", [128, H], F32))
                bd_s = ps.enter_context(nc.sbuf_tensor("bd_s", [128, H], F32))
                dtmp = ps.enter_context(nc.sbuf_tensor("dtmp", [128, 32], F32))
                dtmp2 = ps.enter_context(nc.sbuf_tensor("dtmp2", [128, H * 128], F32))

                sq_st = [ps.enter_context(nc.sbuf_tensor(f"p5q{i}", [128, H * 512], BF16)) for i in range(2)]
                sk_st = [ps.enter_context(nc.sbuf_tensor(f"p5k{i}", [128, H * 512], BF16)) for i in range(2)]
                sv_st = [ps.enter_context(nc.sbuf_tensor(f"p5v{i}", [128, 4 * D], BF16)) for i in range(2)]
                kdT = ps.enter_context(nc.sbuf_tensor("p5kd", [128, H * 128], BF16))
                msc = ps.enter_context(nc.sbuf_tensor("p5ms", [128, H * 128], BF16))
                qd_t = ps.enter_context(nc.sbuf_tensor("p5qd", [128, H * 512], BF16))
                o_st = [ps.enter_context(nc.sbuf_tensor(f"p5o{i}", [128, H * 512], BF16)) for i in range(2)]

                pSO = ps.enter_context(nc.psum_tensor("p5so", [128, 2048], F32))
                pT5 = ps.enter_context(nc.psum_tensor("p5t", [128, 2048], BF16))
                pKV = ps.enter_context(nc.psum_tensor("p5kv", [128, 1024], F32))

                slope_v = sslope[:].rearrange("p (h o) -> p h o", o=1).broadcast_to(
                    [128, H, 128])
                g0, a0, v0 = sG.n, sA.n, sV.n
                with nc.Block("P5s") as blk:
                    @blk.gpsimd
                    def _(gp):
                        ins = gp.iota(dtmp2[:].rearrange("p (h i) -> p h i", h=H),
                                      [[0, H], [-1, 128]], base=0, channel_multiplier=1,
                                      allow_small_or_imprecise_dtypes=True)
                        sG.inc(ins)

                    @blk.vector
                    def _(ve):
                        ve.wait_ge(sG.h, g0 + 1)
                        ins = ve.tensor_tensor(dtmp2[:].rearrange("p (h i) -> p h i", h=H),
                                               dtmp2[:].rearrange("p (h i) -> p h i", h=H),
                                               slope_v, ALU.mult)
                        sV.inc(ins)
                        ve.memset(S_f[:], 0.0)
                        ve.memset(S_b[:], 0.0)

                    @blk.scalar
                    def _(sc):
                        sc.wait_ge(sV.h, v0 + 1)
                        ins = sc.activation(out=mask_a[:], in_=dtmp2[:], func=AF.Exp)
                        sA.inc(ins)

                g1, a1, v1 = sG.n, sA.n, sV.n
                with nc.Block("P5s2") as blk:
                    @blk.gpsimd
                    def _(gp):
                        gp.affine_select(
                            mask_a[:].rearrange("p (h i) -> p h i", h=H),
                            mask_a[:].rearrange("p (h i) -> p h i", h=H),
                            [[0, H], [1, 128]], ALU.is_ge, 0.0,
                            base=0, channel_multiplier=-1)
                        gp.iota(dtmp2[:].rearrange("p (h i) -> p h i", h=H),
                                [[0, H], [-1, 128]], base=0, channel_multiplier=0,
                                allow_small_or_imprecise_dtypes=True)
                        ins = gp.iota(dtmp[:, 0:1], [[0, 1]], base=-128, channel_multiplier=1,
                                      allow_small_or_imprecise_dtypes=True)
                        sG.inc(ins)

                    @blk.vector
                    def _(ve):
                        ve.wait_ge(sG.h, g1 + 1)
                        ve.tensor_tensor(dtmp2[:].rearrange("p (h i) -> p h i", h=H),
                                         dtmp2[:].rearrange("p (h i) -> p h i", h=H),
                                         slope_v, ALU.mult)
                        ins = ve.tensor_tensor(dtmp[:, 16:16 + H],
                                               dtmp[:, 0:1].broadcast_to([128, H]),
                                               sslope[:], ALU.mult)
                        sV.inc(ins)

                    @blk.scalar
                    def _(sc):
                        sc.wait_ge(sV.h, v1 + 1)
                        sc.activation(out=qdec_s[:], in_=dtmp2[:], func=AF.Exp)
                        sc.activation(out=kdec_s[:], in_=dtmp[:, 16:16 + H], func=AF.Exp)
                        ins = sc.activation(out=bd_s[:], in_=sslope[:], func=AF.Exp,
                                            scale=-128.0)
                        sA.inc(ins)

                ld0, st0 = LD.n, ST.n
                a0, p0, v0 = sA.n, sP.n, sV.n
                def peb(t):
                    return p0 + 7 * t

                def dvb(t):
                    return v0 + 5 * t + (t // 4 + 1)

                def acb(t):
                    return a0 + 3 * t

                bd_v = bd_s[:].rearrange("p (h o) -> p h o", o=1).broadcast_to([128, H, 128])
                kd_v = kdec_s[:].rearrange("p (h o) -> p h o", o=1).broadcast_to([128, H, 128])
                qd_v = qdec_s[:].rearrange("p (h o i) -> p h o i", h=H, o=1).broadcast_to(
                    [128, H, 4, 128])

                with nc.Block("P5m") as blk:
                    @blk.sync
                    def _(sy):
                        for gr in range(GR):
                            if gr >= 2:
                                sy.wait_ge(sP.h, peb(4 * (gr - 1)))
                            for h in range(H):
                                sy.dma_start(out=sq_st[gr % 2][:, h * 512:(h + 1) * 512],
                                             in_=q_s[h, :, gr * 512:(gr + 1) * 512]).then_inc(LD.h, 16)
                                sy.dma_start(out=sk_st[gr % 2][:, h * 512:(h + 1) * 512],
                                             in_=k_s[h, :, gr * 512:(gr + 1) * 512]).then_inc(LD.h, 16)
                            sy.dma_start(
                                out=sv_st[gr % 2][:].rearrange("p (c n) -> p c n", c=4),
                                in_=v_s2[gr * 512:(gr + 1) * 512, :].rearrange(
                                    "(c p) n -> p c n", p=128)).then_inc(LD.h, 16)
                    LD.n += 16 * 33 * GR

                    @blk.tensor
                    def _(te):
                        for t in range(CH):
                            gr, c = t // 4, t % 4
                            qo = gr % 2

                            def vsl(h):
                                return sv_st[qo][:, c * D + h * 128: c * D + h * 128 + 128]
                            if c == 0:
                                te.wait_ge(LD.h, ld0 + 16 * 33 * (gr + 1))
                            if t >= 1:
                                te.wait_ge(sV.h, dvb(t - 1) + 1)
                            for h in range(H):
                                ins = te.transpose(
                                    pT5[:, h * 128:(h + 1) * 128],
                                    sk_st[qo][:, h * 512 + c * 128: h * 512 + c * 128 + 128],
                                    sid[:])
                            sP.inc(ins)
                            for half in range(2):
                                if t >= 1:
                                    te.wait_ge(sA.h, acb(t - 1) + 1 + 2 * half)
                                for h in range(8 * half, 8 * half + 8):
                                    ins = te.matmul(
                                        pSO[:, h * 128:(h + 1) * 128],
                                        sk_st[qo][:, h * 512 + c * 128: h * 512 + c * 128 + 128],
                                        sq_st[qo][:, h * 512 + c * 128: h * 512 + c * 128 + 128],
                                        start=True, stop=True)
                                sP.inc(ins)
                            te.wait_ge(sV.h, dvb(t) + 1)
                            if t >= 1:
                                te.wait_ge(sV.h, dvb(t - 1) + 5)
                            for h in range(8):
                                ins = te.matmul(pKV[:, h * 128:(h + 1) * 128],
                                                kdT[:, h * 128:(h + 1) * 128],
                                                vsl(h), start=True, stop=True)
                            sP.inc(ins)
                            te.wait_ge(sV.h, dvb(t) + 2)
                            if t >= 1:
                                te.wait_ge(sA.h, acb(t - 1) + 2)
                            for h in range(8):
                                te.matmul(pSO[:, h * 128:(h + 1) * 128],
                                          vsl(h),
                                          msc[:, h * 128:(h + 1) * 128],
                                          start=True, stop=False)
                                ins = te.matmul(
                                    pSO[:, h * 128:(h + 1) * 128],
                                    S_b[:, h * 128:(h + 1) * 128],
                                    qd_t[:, h * 512 + c * 128: h * 512 + c * 128 + 128],
                                    start=False, stop=True)
                            sP.inc(ins)
                            te.wait_ge(sV.h, dvb(t) + 4)
                            for h in range(8, 16):
                                ins = te.matmul(pKV[:, (h - 8) * 128:(h - 7) * 128],
                                                kdT[:, h * 128:(h + 1) * 128],
                                                vsl(h), start=True, stop=True)
                            sP.inc(ins)
                            te.wait_ge(sV.h, dvb(t) + 3)
                            for h in range(8, 16):
                                te.matmul(pSO[:, h * 128:(h + 1) * 128],
                                          vsl(h),
                                          msc[:, h * 128:(h + 1) * 128],
                                          start=True, stop=False)
                                ins = te.matmul(
                                    pSO[:, h * 128:(h + 1) * 128],
                                    S_b[:, h * 128:(h + 1) * 128],
                                    qd_t[:, h * 512 + c * 128: h * 512 + c * 128 + 128],
                                    start=False, stop=True)
                            sP.inc(ins)

                    @blk.vector
                    def _(ve):
                        for t in range(CH):
                            gr, c = t // 4, t % 4
                            if c == 0:
                                ve.wait_ge(LD.h, ld0 + 16 * 33 * (gr + 1))
                                if gr >= 1:
                                    ve.wait_ge(sP.h, peb(4 * gr))
                                ins = ve.tensor_tensor(
                                    qd_t[:].rearrange("p (h o i) -> p h o i", h=H, o=4),
                                    sq_st[gr % 2][:].rearrange("p (h o i) -> p h o i", h=H, o=4),
                                    qd_v, ALU.mult)
                                sV.inc(ins)
                            ve.wait_ge(sP.h, peb(t) + 1)
                            if t >= 1:
                                ve.wait_ge(sP.h, peb(t - 1) + 6)
                            ins = ve.tensor_tensor(
                                kdT[:].rearrange("p (h i) -> p h i", h=H),
                                pT5[:].rearrange("p (h i) -> p h i", h=H),
                                kd_v, ALU.mult)
                            sV.inc(ins)
                            ve.wait_ge(sP.h, peb(t) + 2)
                            if t >= 1:
                                ve.wait_ge(sP.h, peb(t - 1) + 5)
                            ins = ve.tensor_mul(msc[:, 0:1024], pSO[:, 0:1024],
                                                mask_a[:, 0:1024])
                            sV.inc(ins)
                            ve.wait_ge(sP.h, peb(t) + 3)
                            if t >= 1:
                                ve.wait_ge(sP.h, peb(t - 1) + 7)
                            ins = ve.tensor_mul(msc[:, 1024:2048], pSO[:, 1024:2048],
                                                mask_a[:, 1024:2048])
                            sV.inc(ins)
                            ve.wait_ge(sP.h, peb(t) + 4)
                            if t >= 1:
                                ve.wait_ge(sA.h, acb(t - 1) + 2)
                            ve.tensor_tensor(
                                S_f[:, 0:1024].rearrange("p (h i) -> p h i", h=8),
                                S_f[:, 0:1024].rearrange("p (h i) -> p h i", h=8),
                                bd_v[:, 0:8], ALU.mult)
                            ins = ve.tensor_add(S_f[:, 0:1024], S_f[:, 0:1024], pKV[:])
                            sV.inc(ins)
                            ve.wait_ge(sP.h, peb(t) + 6)
                            ve.tensor_tensor(
                                S_f[:, 1024:2048].rearrange("p (h i) -> p h i", h=8),
                                S_f[:, 1024:2048].rearrange("p (h i) -> p h i", h=8),
                                bd_v[:, 8:16], ALU.mult)
                            ins = ve.tensor_add(S_f[:, 1024:2048], S_f[:, 1024:2048], pKV[:])
                            sV.inc(ins)

                    @blk.scalar
                    def _(sc):
                        for t in range(CH):
                            gr, c = t // 4, t % 4
                            sc.wait_ge(sP.h, peb(t) + 5)
                            if t >= 8:
                                sc.wait_ge(ST.h, st0 + 16 * 16 * (gr - 1))
                            ins = sc.activation(
                                out=o_st[gr % 2][:].rearrange("p (h i) -> p h i", h=H)[
                                    :, 0:8, c * 128:(c + 1) * 128],
                                in_=pSO[:, 0:1024].rearrange("p (h i) -> p h i", h=8),
                                func=AF.Copy)
                            sA.inc(ins)
                            sc.wait_ge(sV.h, dvb(t) + 5)
                            sc.wait_ge(sP.h, peb(t) + 7)
                            ins = sc.activation(out=S_b[:], in_=S_f[:], func=AF.Copy)
                            sA.inc(ins)
                            ins = sc.activation(
                                out=o_st[gr % 2][:].rearrange("p (h i) -> p h i", h=H)[
                                    :, 8:16, c * 128:(c + 1) * 128],
                                in_=pSO[:, 1024:2048].rearrange("p (h i) -> p h i", h=8),
                                func=AF.Copy)
                            sA.inc(ins)

                    @blk.gpsimd
                    def _(gp):
                        for gr in range(GR):
                            gp.wait_ge(sA.h, acb(gr * 4 + 3) + 3)
                            for h in range(H):
                                gp.dma_start(out=a_s[h, :, gr * 512:(gr + 1) * 512],
                                             in_=o_st[gr % 2][:, h * 512:(h + 1) * 512]
                                             ).then_inc(ST.h, 16)
                    ST.n += 16 * 16 * GR

                stc = ST.n
                with nc.Block("P5c") as blk:
                    @blk.gpsimd
                    def _(gp):
                        gp.wait_ge(sV.h, sV.n)
                        gp.dma_start(out=cc_in.rearrange("(h d) e -> d h e", d=128),
                                     in_=S_f[:].rearrange("p (h e) -> p h e", h=H)
                                     ).then_inc(ST.h, 16)
                        gp.wait_ge(ST.h, stc + 16)
                        gp.collective_compute(
                            "AllGather", ALU.bypass,
                            replica_groups=[list(range(N_CORES))],
                            ins=[cc_in.opt()], outs=[cc_out.opt()],
                        ).then_inc(CC.h, 1)
                ST.n += 16
                CC.n += 1

        if "P5" in phases:
            fence("f5", cc=True)

        if "P4" in phases:
            NQ = 4
            KG = 11
            with ExitStack() as ps:
                acc = [ps.enter_context(nc.sbuf_tensor(f"p4a{i}", [128, NT], F32)) for i in range(4)]
                gu_st = [[ps.enter_context(nc.sbuf_tensor(f"p4g{j}_{i}", [128, NT], BF16))
                          for i in range(4)] for j in range(2)]
                wd_b = [ps.enter_context(nc.sbuf_tensor(f"p4wb{i}", [128, 2048], BF16)) for i in range(2)]
                gsil = ps.enter_context(nc.sbuf_tensor("p4gs", [128, 4 * NT], BF16))
                gtm_t = [ps.enter_context(nc.sbuf_tensor(f"p4t{i}", [128, 2048], BF16)) for i in range(2)]
                pD = [ps.enter_context(nc.psum_tensor(f"p4p{i}", [128, 512], F32)) for i in range(2)]
                pTr = [ps.enter_context(nc.psum_tensor(f"p4q{i}", [128, 512], BF16)) for i in range(4)]

                ld0, st0, a0, b0, p0, q0, v0, w0 = LD.n, ST.n, sA.n, sB.n, sP.n, sQ.n, sV.n, sW.n
                g4 = sG.n
                def mmi(dq, kg, dc, g):
                    return p0 + ((dq * KG + kg) * 4 + dc) * 8 + g + 1

                def addi(dq, kg, dc, g):
                    return v0 + ((dq * KG + kg) * 4 + dc) * 8 + g + 1

                gsv = gsil[:].rearrange("p (dc n) -> p dc n", dc=4)

                with nc.Block("P4") as blk:
                    @blk.sync
                    def _(sy):
                        for dq in range(NQ):
                            for kg in range(KG):
                                u = dq * KG + kg
                                if u >= 2:
                                    dqp, kgp = divmod(u - 2, KG)
                                    sy.wait_ge(sP.h, mmi(dqp, kgp, 3, 7))
                                for i in range(4):
                                    sy.dma_start(out=gu_st[u % 2][i][:],
                                                 in_=gu_s[kg * 4 + i, :, :]).then_inc(LD.h, 16)
                    LD.n += 16 * 4 * NQ * KG

                    @blk.scalar
                    def _(sc):
                        def gevac(dqq):
                            for tile in range(TT):
                                k6 = dqq * TT + tile
                                sc.wait_ge(sQ.h, q0 + k6 + 1)
                                if k6 >= 8:
                                    sc.wait_ge(ST.h, st0 + 16 * (k6 // 4 - 1))
                                ins = sc.activation(
                                    out=gtm_t[(tile // 4) % 2][:, (tile % 4) * 512:
                                                               (tile % 4) * 512 + 512],
                                    in_=pTr[tile % 4][:], func=AF.Copy)
                                sW.inc(ins)
                                if tile % 4 == 3:
                                    sc.wait_ge(sW.h, sW.n)
                                    sc.dma_start(
                                        out=g_tm[(tile - 3) * 128:(tile + 1) * 128,
                                                 dqq * 512:(dqq + 1) * 512].rearrange(
                                                     "(c p) n -> p c n", p=128),
                                        in_=gtm_t[(tile // 4) % 2][:].rearrange(
                                            "p (c n) -> p c n", c=4)).then_inc(ST.h, 16)
                        for dq in range(NQ):
                            for kg in range(KG):
                                u = dq * KG + kg
                                sc.wait_ge(sG.h, g4 + 16 * (u + 1))
                                ins = sc.activation(out=wd_b[u % 2][:], in_=wd_b[u % 2][:],
                                                    func=AF.Copy, bias=-128.0)
                                sA.inc(ins)
                            if dq >= 1:
                                gevac(dq - 1)
                            sc.wait_ge(sV.h, addi(dq, KG - 1, 3, 7))
                            if dq >= 1:
                                sc.wait_ge(sQ.h, q0 + dq * TT)
                            for dc in range(4):
                                ins = sc.activation(out=gsv[:, dc, :], in_=acc[dc][:],
                                                    func=AF.Silu,
                                                    scale=ssd[:, dq * 4 + dc:dq * 4 + dc + 1])
                            sB.inc(ins)
                        gevac(NQ - 1)

                    @blk.tensor
                    def _(te):
                        def trans_quarter(dq):
                            te.wait_ge(sB.h, b0 + dq + 1)
                            for tile in range(TT):
                                if dq * TT + tile >= 4:
                                    te.wait_ge(sW.h, w0 + dq * TT + tile - 3)
                                for dc in range(4):
                                    ins = te.transpose(
                                        pTr[tile % 4][:, dc * 128:(dc + 1) * 128],
                                        gsv[:, dc, tile * 128:(tile + 1) * 128],
                                        sid[:])
                                sQ.inc(ins)
                        for dq in range(NQ):
                            for kg in range(KG):
                                u = dq * KG + kg
                                te.wait_ge(sA.h, a0 + u + 1)
                                te.wait_ge(LD.h, ld0 + 16 * 4 * (u + 1))
                                for dc in range(4):
                                    for g in range(8):
                                        j = (dc * 8 + g) % 2
                                        if u > 0 or dc * 8 + g >= 2:
                                            te.wait_ge(sV.h, addi(dq, kg, dc, g) - 2)
                                        for i in range(4):
                                            ins = te.matmul(
                                                pD[j][:],
                                                wd_b[u % 2][:, i * 512 + dc * 128:
                                                             i * 512 + dc * 128 + 128],
                                                gu_st[u % 2][i][:, g * 512:(g + 1) * 512],
                                                start=(i == 0), stop=(i == 3))
                                        sP.inc(ins)
                            if dq >= 1:
                                trans_quarter(dq - 1)
                        trans_quarter(NQ - 1)

                    @blk.vector
                    def _(ve):
                        for dq in range(NQ):
                            for kg in range(KG):
                                for dc in range(4):
                                    for g in range(8):
                                        j = (dc * 8 + g) % 2
                                        ve.wait_ge(sP.h, mmi(dq, kg, dc, g))
                                        if kg == 0:
                                            if dq >= 1:
                                                ve.wait_ge(sB.h, b0 + dq)
                                            ins = ve.tensor_copy(
                                                acc[dc][:, g * 512:(g + 1) * 512], pD[j][:])
                                        else:
                                            ins = ve.tensor_add(
                                                acc[dc][:, g * 512:(g + 1) * 512],
                                                acc[dc][:, g * 512:(g + 1) * 512], pD[j][:])
                                        sV.inc(ins)


                    @blk.gpsimd
                    def _(gp):
                        for dq in range(NQ):
                            for kg in range(KG):
                                u = dq * KG + kg
                                if u >= 2:
                                    dqp, kgp = divmod(u - 2, KG)
                                    gp.wait_ge(sP.h, mmi(dqp, kgp, 3, 7))
                                gp.dma_start(
                                    out=wd_b[u % 2][:].rearrange("p (a n) -> p a n", a=4),
                                    in_=wdT.rearrange("(a p) n -> p a n", p=128)[
                                        :, kg * 4:kg * 4 + 4, dq * 512:(dq + 1) * 512],
                                ).then_inc(sG.h, 16)
                    sG.n += 16 * NQ * KG
                    ST.n += 16 * NQ * (TT // 4)

        if "P4" in phases:
            fence("f4")

        if "P55" in phases and "P6" in phases:
            with ExitStack() as ps:
                qde = ps.enter_context(nc.sbuf_tensor("p7qe", [128, H * 512], BF16))
                bd5 = ps.enter_context(nc.sbuf_tensor("p7bd", [128, H], F32))
                pS6 = [ps.enter_context(nc.psum_tensor(f"p7p{i}", [128, D], F32)) for i in range(2)]

                with ExitStack() as ps2:
                    dneg = ps2.enter_context(nc.sbuf_tensor("p7i", [128, 513], F32))
                    ld_r = ps2.enter_context(nc.sbuf_tensor("p7ld", [128, H * 128], F32))

                    g0, a00, v00 = sG.n, sA.n, sV.n
                    with nc.Block("P7s") as blk:
                        @blk.gpsimd
                        def _(gp):
                            gp.iota(dneg[:, 0:512], [[-1, 512]], base=0, channel_multiplier=0,
                                    allow_small_or_imprecise_dtypes=True)
                            ins = gp.memset(dneg[:, 512:513], -512.0)
                            sG.inc(ins)

                        @blk.scalar
                        def _(sc):
                            sc.wait_ge(sG.h, g0 + 1)
                            for h in range(H):
                                sc.activation(out=qde[:, h * 512:(h + 1) * 512],
                                              in_=dneg[:, 0:512],
                                              func=AF.Exp, scale=sslope[:, h:h + 1])
                                ins = sc.activation(out=bd5[:, h:h + 1], in_=dneg[:, 512:513],
                                                    func=AF.Exp, scale=sslope[:, h:h + 1])
                            sA.inc(ins)

                        @blk.vector
                        def _(ve):
                            ve.memset(S_f[:], 0.0)

                    ld0 = LD.n
                    v0 = sV.n
                    bd5_v = bd5[:].rearrange("p (h o) -> p h o", o=1).broadcast_to([128, H, 128])
                    with nc.Block("P7a") as blk:
                        @blk.sync
                        def _(sy):
                            sy.wait_ge(CC.h, CC.n)
                            for r in range(N_CORES):
                                if r >= 1:
                                    sy.wait_ge(sV.h, v0 + r)
                                sy.dma_start(
                                    out=ld_r[:].rearrange("p (h e) -> p h e", h=H),
                                    in_=cc_out.rearrange("(r h d) e -> r d h e",
                                                         r=N_CORES, d=128)[r],
                                ).then_inc(LD.h, 16)
                        LD.n += 16 * N_CORES

                        @blk.vector
                        def _(ve):
                            for r in range(N_CORES):
                                ve.wait_ge(LD.h, ld0 + 16 * (r + 1))
                                ins = ve.scalar_tensor_tensor(S_f[:], ld_r[:], ssel[:, r:r + 1],
                                                              S_f[:], ALU.mult, ALU.add)
                                sV.inc(ins)

                        @blk.scalar
                        def _(sc):
                            sc.wait_ge(sV.h, v0 + N_CORES)
                            ins = sc.activation(out=S_b[:], in_=S_f[:], func=AF.Copy)
                            sA.inc(ins)

                    ld0, st0, a0, p0, q0, v0, w0, d0, g6, c0 = (
                        LD.n, ST.n, sA.n, sP.n, sQ.n, sV.n, sW.n, sD.n, sG.n, sC.n)

                wo_b = ps.enter_context(nc.sbuf_tensor("p7wb", [128, DC * D], BF16))
                q5 = [ps.enter_context(nc.sbuf_tensor(f"p7q{i}", [128, H * 512], BF16)) for i in range(2)]
                at5 = [ps.enter_context(nc.sbuf_tensor(f"p7a{i}", [128, H * 512], BF16)) for i in range(2)]
                g_f = [ps.enter_context(nc.sbuf_tensor(f"p7gf{i}", [128, D], F32)) for i in range(2)]
                sgp = ps.enter_context(nc.sbuf_tensor("p7s", [128, D], F32))
                x_st6 = [ps.enter_context(nc.sbuf_tensor(f"p7x{i}", [128, D], F32)) for i in range(2)]

                with nc.Block("P7m") as blk:
                    @blk.sync
                    def _(sy):
                        for g in range(GR):
                            if g >= 2:
                                sy.wait_ge(sP.h, p0 + 4 * (g - 1))
                                sy.wait_ge(sQ.h, q0 + 4 * (g - 1))
                            for h in range(H):
                                sy.dma_start(out=at5[g % 2][:, h * 512:(h + 1) * 512],
                                             in_=a_s[h, :, g * 512:(g + 1) * 512]
                                             ).then_inc(LD.h, 16)
                                sy.dma_start(out=q5[g % 2][:, h * 512:(h + 1) * 512],
                                             in_=q_s[h, :, g * 512:(g + 1) * 512]
                                             ).then_inc(LD.h, 16)
                            for tl in range(4):
                                t = g * 4 + tl
                                if t >= 2:
                                    sy.wait_ge(ST.h, st0 + 16 * (t - 1))
                                sy.dma_start(out=x_st6[t % 2][:],
                                             in_=x_in[t * 128:(t + 1) * 128, :]
                                             ).then_inc(LD.h, 16)
                    LD.n += 16 * 36 * GR

                    @blk.gpsimd
                    def _(gp):
                        gp.dma_start(
                            out=wo_b[:].rearrange("p (c n) -> p c n", c=DC),
                            in_=wout.rearrange("(c p) n -> p c n", p=128),
                        ).then_inc(sG.h, 16)
                        for t in range(TT):
                            if t >= 2:
                                gp.wait_ge(sV.h, v0 + t - 1)
                            gp.dma_start(out=g_f[t % 2][:],
                                         in_=g_tm[t * 128:(t + 1) * 128, :]).then_inc(sG.h, 16)
                    sG.n += 16 * (TT + 1)

                    @blk.vector
                    def _(ve):
                        nocorr = "NOCORR" in phases
                        for g in range(GR):
                            ve.wait_ge(LD.h, ld0 + 16 * (36 * g + 32))
                            ins = ve.tensor_mul(q5[g % 2][:], q5[g % 2][:], qde[:])
                            sD.inc(ins)
                            for qq in range(4):
                                if nocorr:
                                    sW.inc(ve.nop())
                                    continue
                                ve.wait_ge(sQ.h, q0 + g * 4 + qq + 1)
                                ins = ve.tensor_add(at5[g % 2][:, qq * 2048:(qq + 1) * 2048],
                                                    pS6[qq % 2][:],
                                                    at5[g % 2][:, qq * 2048:(qq + 1) * 2048])
                                sW.inc(ins)
                            ve.wait_ge(sA.h, a0 + g)
                            ins = ve.tensor_tensor(
                                S_f[:].rearrange("p (h i) -> p h i", h=H),
                                S_f[:].rearrange("p (h i) -> p h i", h=H),
                                bd5_v, ALU.mult)
                            sD.inc(ins)
                            for tl in range(4):
                                t = g * 4 + tl
                                ve.wait_ge(sP.h, p0 + t + 1)
                                ve.wait_ge(sG.h, g6 + 16 * (t + 2))
                                ins = ve.tensor_mul(sgp[:], pS6[t % 2][:], g_f[t % 2][:])
                                sV.inc(ins)
                                ve.wait_ge(LD.h, ld0 + 16 * (36 * g + 32 + tl + 1))
                                ins = ve.tensor_add(x_st6[t % 2][:], sgp[:], x_st6[t % 2][:])
                                sC.inc(ins)

                    @blk.scalar
                    def _(sc):
                        for g in range(GR):
                            if g >= 1:
                                sc.wait_ge(sD.h, d0 + 2 * g)
                                sc.wait_ge(sQ.h, q0 + 4 * g)
                                ins = sc.activation(out=S_b[:], in_=S_f[:], func=AF.Copy)
                                sA.inc(ins)
                            for tl in range(4):
                                t = g * 4 + tl
                                sc.wait_ge(sC.h, c0 + t + 1)
                                sc.dma_start(out=out[t * 128:(t + 1) * 128, :],
                                             in_=x_st6[t % 2][:]).then_inc(ST.h, 16)
                    ST.n += 16 * TT

                    @blk.tensor
                    def _(te):
                        te.wait_ge(sG.h, g6 + 16)
                        for g in range(GR):
                            te.wait_ge(sD.h, d0 + 2 * g + 1)
                            te.wait_ge(sA.h, a0 + g)
                            if g >= 1:
                                te.wait_ge(sV.h, v0 + 4 * g)
                            for qq in range(4):
                                if "NOCORR" in phases:
                                    sQ.inc(te.nop())
                                    continue
                                if qq >= 2:
                                    te.wait_ge(sW.h, w0 + 4 * g + qq - 1)
                                for hh in range(4):
                                    h = qq * 4 + hh
                                    ins = te.matmul(pS6[qq % 2][:, hh * 512:(hh + 1) * 512],
                                                    S_b[:, h * 128:(h + 1) * 128],
                                                    q5[g % 2][:, h * 512:(h + 1) * 512],
                                                    start=True, stop=True)
                                sQ.inc(ins)
                            te.wait_ge(sW.h, w0 + 4 * (g + 1))
                            for tl in range(4):
                                t = g * 4 + tl
                                if t >= 2:
                                    te.wait_ge(sV.h, v0 + t - 1)
                                for nn in range(4):
                                    for cc in range(DC):
                                        te.matmul(
                                            pS6[t % 2][:, nn * 512:(nn + 1) * 512],
                                            at5[g % 2][:, cc * 512 + tl * 128:
                                                         cc * 512 + tl * 128 + 128],
                                            wo_b[:, cc * D + nn * 512: cc * D + nn * 512 + 512],
                                            start=(cc == 0), stop=False)
                                    ins = te.matmul(pS6[t % 2][:, nn * 512:(nn + 1) * 512],
                                                    sones[:, 0:128],
                                                    sbout_b[:, nn * 512:(nn + 1) * 512],
                                                    start=False, stop=True)
                                sP.inc(ins)

        with nc.Block("drain") as blk:
            @blk.gpsimd
            def _(gp):
                gp.wait_ge(ST.h, ST.n)
            @blk.sync
            def _(sy):
                sy.wait_ge(LD.h, LD.n)

    return nc


def _prep_inputs(L0_attn_x, L0_attn_Wqkv, L0_attn_bqkv, L0_attn_Wout, L0_attn_bout,
                 L0_attn_slopes, L0_ffn_wq_gate, L0_ffn_s_gate, L0_ffn_wq_up,
                 L0_ffn_s_up, L0_ffn_wq_down, L0_ffn_s_down):
    x = np.ascontiguousarray(np.asarray(L0_attn_x, dtype=np.float32)).reshape(4 * 8192, D)
    shared = {
        "wqkv": np.ascontiguousarray(np.asarray(L0_attn_Wqkv, dtype=np.float32)),
        "bqkv_c": np.ascontiguousarray(np.asarray(L0_attn_bqkv, np.float32).reshape(48, 128).T),
        "wout": np.ascontiguousarray(np.asarray(L0_attn_Wout, dtype=np.float32)),
        "bout_r": np.ascontiguousarray(np.asarray(L0_attn_bout, np.float32).reshape(1, D)),
        "slope_b": np.ascontiguousarray(
            np.broadcast_to(np.asarray(L0_attn_slopes, np.float32)[None, :], (128, H))),
        "wgT": np.ascontiguousarray(np.asarray(L0_ffn_wq_gate, np.int32).T),
        "wuT": np.ascontiguousarray(np.asarray(L0_ffn_wq_up, np.int32).T),
        "wdT": np.ascontiguousarray(np.asarray(L0_ffn_wq_down, np.int32).T),
        "sg_c": np.ascontiguousarray(np.asarray(L0_ffn_s_gate, np.float32).reshape(HC, 128).T),
        "su_c": np.ascontiguousarray(np.asarray(L0_ffn_s_up, np.float32).reshape(HC, 128).T),
        "sd_c": np.ascontiguousarray(np.asarray(L0_ffn_s_down, np.float32).reshape(DC, 128).T),
        "ident": np.eye(128, dtype=ml_dtypes.bfloat16),
        "ones_b": np.ones((1, 128), dtype=ml_dtypes.bfloat16),
        "ones_f": np.ones((1, 128), dtype=np.float32),
        "bqv_row": np.ascontiguousarray(
            np.asarray(L0_attn_bqkv, np.float32)[2 * D:].reshape(1, D)),
    }
    in_maps = []
    for c in range(N_CORES):
        sel = np.zeros((128, N_CORES), dtype=np.float32)
        if c % 2 == 1:
            sel[:, c - 1] = 1.0
        m = dict(shared)
        m["x"] = x[c * NT:(c + 1) * NT]
        m["selm"] = sel
        in_maps.append(m)
    return in_maps


def kernel(**inputs):
    in_maps = _prep_inputs(**inputs)
    nc = build_program()
    res = run_bass_kernel_spmd(nc, in_maps, core_ids=list(range(N_CORES)))
    outs = [r["out"] for r in res.results]
    return np.concatenate(outs, axis=0).reshape(4, 8192, D)


# version 17
# speedup vs baseline: 1.0234x; 1.0070x over previous
import numpy as np
import ml_dtypes
import concourse.bass as bass
import concourse.mybir as mybir
from concourse.bass_utils import run_bass_kernel_spmd
from contextlib import ExitStack

F32 = mybir.dt.float32
BF16 = mybir.dt.bfloat16
I32 = mybir.dt.int32
AF = mybir.ActivationFunctionType
ALU = mybir.AluOpType
AX = mybir.AxisListType

D = 2048
DC = 16
NT = 4096
TT = 32
H = 16
HID = 5632
HC = 44
CH = 32
GR = 8
EPS = 1e-5
N_CORES = 8


class Sem:
    def __init__(self, handle):
        self.h = handle
        self.n = 0

    def inc(self, instr, k=1):
        instr.then_inc(self.h, k)
        self.n += k
        return self.n


def build_program(phases=("P1", "P2", "P3", "P5", "P4", "P55", "P6"), dbg=()):
    nc = bass.Bass("TRN2", target_bir_lowering=False)

    def scratch(name, shape, dt):
        kind = "ExternalOutput" if name in dbg else "Internal"
        return nc.dram_tensor(name, shape, dt, kind=kind).ap()

    x_in = nc.dram_tensor("x", [NT, D], F32, kind="ExternalInput").ap()
    wqkv = nc.dram_tensor("wqkv", [D, 3 * D], F32, kind="ExternalInput").ap()
    bqkv_c = nc.dram_tensor("bqkv_c", [128, 48], F32, kind="ExternalInput").ap()
    wout = nc.dram_tensor("wout", [D, D], F32, kind="ExternalInput").ap()
    bout_r = nc.dram_tensor("bout_r", [1, D], F32, kind="ExternalInput").ap()
    slope_b = nc.dram_tensor("slope_b", [128, H], F32, kind="ExternalInput").ap()
    wgT = nc.dram_tensor("wgT", [D, HID], I32, kind="ExternalInput").ap()
    wuT = nc.dram_tensor("wuT", [D, HID], I32, kind="ExternalInput").ap()
    wdT = nc.dram_tensor("wdT", [HID, D], I32, kind="ExternalInput").ap()
    sg_c = nc.dram_tensor("sg_c", [128, HC], F32, kind="ExternalInput").ap()
    su_c = nc.dram_tensor("su_c", [128, HC], F32, kind="ExternalInput").ap()
    sd_c = nc.dram_tensor("sd_c", [128, DC], F32, kind="ExternalInput").ap()
    ident = nc.dram_tensor("ident", [128, 128], BF16, kind="ExternalInput").ap()
    ones_b = nc.dram_tensor("ones_b", [1, 128], BF16, kind="ExternalInput").ap()
    ones_f = nc.dram_tensor("ones_f", [1, 128], F32, kind="ExternalInput").ap()
    bqv_row = nc.dram_tensor("bqv_row", [1, D], F32, kind="ExternalInput").ap()
    selm = nc.dram_tensor("selm", [128, N_CORES], F32, kind="ExternalInput").ap()
    out = nc.dram_tensor("out", [NT, D], F32, kind="ExternalOutput").ap()

    pad0 = nc.dram_tensor("pad0", [4096, 1024], F32).ap()
    cc_in = nc.dram_tensor("cc_in", [H * 128, 128], F32).ap()
    pad1 = nc.dram_tensor("pad1", [4096, 1024], F32).ap()
    cc_out = nc.dram_tensor("cc_out", [N_CORES * H * 128, 128], F32).ap()
    pad2 = nc.dram_tensor("pad2", [4096, 1024], F32).ap()
    q_s = scratch("q_s", [H, 128, NT], BF16)
    k_s = scratch("k_s", [H, 128, NT], BF16)
    v_s2 = scratch("v_s2", [NT, D], BF16)
    gu_s = scratch("gu_s", [HC, 128, NT], BF16)
    a_s = scratch("a_s", [H, 128, NT], BF16)
    g_tm = scratch("g_tm", [NT, D], BF16)
    s_dbg = scratch("s_dbg", [NT, D], F32)

    es = ExitStack()

    def sb(name, shape, dt):
        return es.enter_context(nc.sbuf_tensor(name, shape, dt))

    def sem(name):
        return Sem(es.enter_context(nc.semaphore(name)))

    with es:
        sid = sb("sid", [128, 128], BF16)
        sones = sb("sones", [1, 128], BF16)
        sonesf = sb("sonesf", [1, 128], F32)
        sbq = sb("sbq", [128, 48], F32)
        sbout_b = sb("sbout_b", [1, D], BF16)
        sslope = sb("sslope", [128, H], F32)
        ssg = sb("ssg", [128, HC], F32)
        ssu = sb("ssu", [128, HC], F32)
        ssd = sb("ssd", [128, DC], F32)
        ssel = sb("ssel", [128, N_CORES], F32)
        S_f = sb("S_f", [128, H * 128], F32)
        S_b = sb("S_b", [128, H * 128], BF16)

        LD = sem("LD")
        ST = sem("ST")
        CC = sem("CC")
        sA = sem("sA")
        sB = sem("sB")
        sC = sem("sC")
        sD = sem("sD")
        sV = sem("sV")
        sW = sem("sW")
        sP = sem("sP")
        sQ = sem("sQ")
        sG = sem("sG")

        def fence(name, cc=False):
            with nc.Block(name) as blk:
                @blk.gpsimd
                def _(gp):
                    gp.wait_ge(ST.h, ST.n)
                    if cc:
                        gp.wait_ge(CC.h, CC.n)
                @blk.sync
                def _(sy):
                    sy.wait_ge(LD.h, LD.n)

        with ExitStack() as ps0:
            sbout = ps0.enter_context(nc.sbuf_tensor("sbout", [1, D], F32))
            with nc.Block("setup") as blk:
                @blk.sync
                def _(sy):
                    sy.dma_start(out=sid[:], in_=ident).then_inc(LD.h, 16)
                    sy.dma_start(out=sones[:], in_=ones_b).then_inc(LD.h, 16)
                    sy.dma_start(out=sonesf[:], in_=ones_f).then_inc(LD.h, 16)
                    sy.dma_start(out=sbq[:], in_=bqkv_c).then_inc(LD.h, 16)
                    sy.dma_start(out=sbout[:], in_=bout_r).then_inc(LD.h, 16)
                    sy.dma_start(out=sslope[:], in_=slope_b).then_inc(LD.h, 16)
                    sy.dma_start(out=ssg[:], in_=sg_c).then_inc(LD.h, 16)
                    sy.dma_start(out=ssu[:], in_=su_c).then_inc(LD.h, 16)
                    sy.dma_start(out=ssd[:], in_=sd_c).then_inc(LD.h, 16)
                    sy.dma_start(out=ssel[:], in_=selm).then_inc(LD.h, 16)
                LD.n += 16 * 10

                @blk.scalar
                def _(sc):
                    sc.wait_ge(LD.h, LD.n)
                    ins = sc.activation(out=sbout_b[:], in_=sbout[:], func=AF.Copy)
                    sA.inc(ins)

        es_xn = ExitStack()
        xn_fm = es_xn.enter_context(nc.sbuf_tensor("xn_fm", [128, DC * NT], BF16))

        if "P1" in phases:
            with ExitStack() as ps:
                x_st = [ps.enter_context(nc.sbuf_tensor(f"p1x{i}", [128, D], F32)) for i in range(2)]
                xsq = ps.enter_context(nc.sbuf_tensor("p1sq", [128, D], F32))
                xn_tm = [ps.enter_context(nc.sbuf_tensor(f"p1n{i}", [128, D], BF16)) for i in range(2)]
                sums = ps.enter_context(nc.sbuf_tensor("p1sums", [128, TT], F32))
                ssqs = ps.enter_context(nc.sbuf_tensor("p1ssqs", [128, TT], F32))
                mean = ps.enter_context(nc.sbuf_tensor("p1mean", [128, TT], F32))
                e2 = ps.enter_context(nc.sbuf_tensor("p1e2", [128, TT], F32))
                msq = ps.enter_context(nc.sbuf_tensor("p1msq", [128, TT], F32))
                rstd = ps.enter_context(nc.sbuf_tensor("p1rs", [128, TT], F32))
                nmr = ps.enter_context(nc.sbuf_tensor("p1nm", [128, TT], F32))
                pT = [ps.enter_context(nc.psum_tensor(f"p1p{i}", [128, D], BF16)) for i in range(2)]

                ld0 = LD.n
                a0, v0, w0, d0, c0, b0, p0, q0 = sA.n, sV.n, sW.n, sD.n, sC.n, sB.n, sP.n, sQ.n
                with nc.Block("P1") as blk:
                    @blk.sync
                    def _(sy):
                        for t in range(TT):
                            if t >= 2:
                                sy.wait_ge(sB.h, b0 + t - 1)
                            sy.dma_start(out=x_st[t % 2][:],
                                         in_=x_in[t * 128:(t + 1) * 128, :]).then_inc(LD.h, 16)
                    LD.n += 16 * TT

                    @blk.scalar
                    def _(sc):
                        for t in range(TT):
                            sc.wait_ge(LD.h, ld0 + 16 * (t + 1))
                            ins = sc.activation(out=xsq[:], in_=x_st[t % 2][:], func=AF.Square,
                                                accum_out=ssqs[:, t:t + 1])
                            sA.inc(ins)
                            sc.wait_ge(sW.h, w0 + t + 1)
                            sc.activation(out=e2[:, t:t + 1], in_=e2[:, t:t + 1],
                                          func=AF.Ln)
                            ins = sc.activation(out=rstd[:, t:t + 1], in_=e2[:, t:t + 1],
                                                func=AF.Exp, scale=-0.5)
                            sD.inc(ins)
                            sc.wait_ge(sC.h, c0 + t + 1)
                            if t >= 2:
                                sc.wait_ge(sP.h, p0 + t - 1)
                            ins = sc.activation(out=xn_tm[t % 2][:], in_=x_st[t % 2][:],
                                                func=AF.Identity, scale=rstd[:, t:t + 1],
                                                bias=nmr[:, t:t + 1])
                            sB.inc(ins)

                    @blk.vector
                    def _(ve):
                        for t in range(TT):
                            ve.wait_ge(LD.h, ld0 + 16 * (t + 1))
                            ins = ve.reduce_sum(sums[:, t:t + 1], x_st[t % 2][:], axis=AX.X)
                            sV.inc(ins)
                            ve.wait_ge(sA.h, a0 + t + 1)
                            ve.tensor_scalar_mul(mean[:, t:t + 1], sums[:, t:t + 1], 1.0 / D)
                            ve.tensor_scalar_mul(e2[:, t:t + 1], ssqs[:, t:t + 1], 1.0 / D)
                            ve.tensor_mul(msq[:, t:t + 1], mean[:, t:t + 1], mean[:, t:t + 1])
                            ve.tensor_sub(e2[:, t:t + 1], e2[:, t:t + 1], msq[:, t:t + 1])
                            ins = ve.tensor_scalar_add(e2[:, t:t + 1], e2[:, t:t + 1], EPS)
                            sW.inc(ins)
                            ve.wait_ge(sD.h, d0 + t + 1)
                            ins = ve.scalar_tensor_tensor(nmr[:, t:t + 1], mean[:, t:t + 1],
                                                          -1.0, rstd[:, t:t + 1],
                                                          ALU.mult, ALU.mult)
                            sC.inc(ins)
                            if t >= 1:
                                ve.wait_ge(sP.h, p0 + t)
                                ins = ve.tensor_copy(
                                    xn_fm[:].rearrange("p (c n) -> p c n", c=DC)[
                                        :, :, (t - 1) * 128:t * 128],
                                    pT[(t - 1) % 2][:].rearrange("p (c i) -> p c i", c=DC))
                                sQ.inc(ins)
                        ve.wait_ge(sP.h, p0 + TT)
                        ins = ve.tensor_copy(
                            xn_fm[:].rearrange("p (c n) -> p c n", c=DC)[
                                :, :, (TT - 1) * 128:TT * 128],
                            pT[(TT - 1) % 2][:].rearrange("p (c i) -> p c i", c=DC))
                        sQ.inc(ins)

                    @blk.tensor
                    def _(te):
                        for t in range(TT):
                            te.wait_ge(sB.h, b0 + t + 1)
                            if t >= 2:
                                te.wait_ge(sQ.h, q0 + t - 1)
                            for d in range(DC):
                                ins = te.transpose(pT[t % 2][:, d * 128:(d + 1) * 128],
                                                   xn_tm[t % 2][:, d * 128:(d + 1) * 128],
                                                   sid[:])
                            sP.inc(ins)

        if "P1" in phases:
            fence("f1")

        if "P2" in phases:
            with ExitStack() as ps:
                wbf = [ps.enter_context(nc.sbuf_tensor(f"p2b{i}", [128, D], BF16)) for i in range(2)]
                ev = [ps.enter_context(nc.sbuf_tensor(f"p2e{i}", [128, 512], BF16)) for i in range(4)]
                pQ = [ps.enter_context(nc.psum_tensor(f"p2p{i}", [128, 512], F32)) for i in range(4)]

                ld0, st0, a0, b0, p0 = LD.n, ST.n, sA.n, sB.n, sP.n
                with nc.Block("P2a") as blk:
                    @blk.gpsimd
                    def _(gp):
                        for cg in range(32):
                            if cg >= 2:
                                gp.wait_ge(sP.h, p0 + 8 * (cg - 1))
                            gp.dma_start(
                                out=wbf[cg % 2][:].rearrange("p (c m) -> p c m", c=DC),
                                in_=wqkv[:, cg * 128:(cg + 1) * 128].rearrange(
                                    "(c p) m -> p c m", p=128),
                            ).then_inc(LD.h, 16)
                    LD.n += 16 * 32

                    @blk.scalar
                    def _(sc):
                        for cg in range(32):
                            for g in range(8):
                                k = cg * 8 + g
                                sc.wait_ge(sP.h, p0 + k + 1)
                                if k >= 4:
                                    sc.wait_ge(ST.h, st0 + 16 * (k - 3))
                                ins = sc.activation(out=ev[k % 4][:], in_=pQ[k % 4][:],
                                                    func=AF.Identity, bias=sbq[:, cg:cg + 1])
                                sB.inc(ins)

                    @blk.tensor
                    def _(te):
                        for cg in range(32):
                            te.wait_ge(LD.h, ld0 + 16 * (cg + 1))
                            for g in range(8):
                                k = cg * 8 + g
                                if k >= 4:
                                    te.wait_ge(sB.h, b0 + k - 3)
                                for d in range(DC):
                                    ins = te.matmul(
                                        pQ[k % 4][:],
                                        wbf[cg % 2][:, d * 128:(d + 1) * 128],
                                        xn_fm[:, d * NT + g * 512: d * NT + g * 512 + 512],
                                        start=(d == 0), stop=(d == DC - 1))
                                sP.inc(ins)

                    @blk.sync
                    def _(sy):
                        for cg in range(32):
                            typ, h = divmod(cg, H)
                            dst = (q_s, k_s)[typ]
                            for g in range(8):
                                k = cg * 8 + g
                                sy.wait_ge(sB.h, b0 + k + 1)
                                sy.dma_start(out=dst[h, :, g * 512:(g + 1) * 512],
                                             in_=ev[k % 4][:]).then_inc(ST.h, 16)
                    ST.n += 16 * 8 * 32

        if "P2" in phases:
            fence("f2a")

        if "P2" in phases:
            with ExitStack() as ps:
                wv_bf = [ps.enter_context(nc.sbuf_tensor(f"p2vb{i}", [128, 16 * 512], BF16)) for i in range(2)]
                bqv_r = ps.enter_context(nc.sbuf_tensor("p2bqr", [1, D], F32))
                bqv = ps.enter_context(nc.sbuf_tensor("p2bqv", [128, D], F32))
                ev2 = [ps.enter_context(nc.sbuf_tensor(f"p2e2{i}", [128, 512], BF16)) for i in range(4)]
                pQ2 = [ps.enter_context(nc.psum_tensor(f"p2q2{i}", [128, 512], F32)) for i in range(4)]

                ld0, st0, a0, p0, q0, v0, w0 = LD.n, ST.n, sA.n, sP.n, sQ.n, sV.n, sW.n
                with nc.Block("P2b") as blk:
                    @blk.sync
                    def _(sy):
                        sy.dma_start(out=bqv_r[:], in_=bqv_row).then_inc(LD.h, 16)
                        for vg in range(4):
                            for tt in range(TT):
                                k = vg * TT + tt
                                sy.wait_ge(sW.h, w0 + k + 1)
                                sy.dma_start(
                                    out=v_s2[tt * 128:(tt + 1) * 128,
                                             vg * 512:(vg + 1) * 512],
                                    in_=ev2[k % 4][:]).then_inc(ST.h, 16)
                    ST.n += 16 * 128

                    @blk.gpsimd
                    def _(gp):
                        for vg in range(4):
                            if vg >= 2:
                                gp.wait_ge(sP.h, p0 + TT * (vg - 1))
                            ins = gp.dma_start(
                                out=wv_bf[vg % 2][:].rearrange("p (c n) -> p c n", c=DC),
                                in_=wqkv[:, 2 * D + vg * 512: 2 * D + (vg + 1) * 512
                                         ].rearrange("(c p) n -> p c n", p=128))
                            ins.then_inc(LD.h, 16)
                    LD.n += 16 * 5

                    @blk.tensor
                    def _(te):
                        te.wait_ge(LD.h, ld0 + 16)
                        for i in range(4):
                            ins = te.matmul(pQ2[i][:], sonesf[:, 0:128],
                                            bqv_r[:, i * 512:(i + 1) * 512],
                                            start=True, stop=True)
                        sQ.inc(ins)
                        te.wait_ge(sV.h, v0 + 1)
                        for vg in range(4):
                            te.wait_ge(LD.h, ld0 + 16 * (vg + 2))
                            for tt in range(TT):
                                k = vg * TT + tt
                                if k >= 4:
                                    te.wait_ge(sW.h, w0 + k - 3)
                                for d in range(DC):
                                    ins = te.matmul(
                                        pQ2[k % 4][:],
                                        xn_fm[:, d * NT + tt * 128: d * NT + tt * 128 + 128],
                                        wv_bf[vg % 2][:, d * 512:(d + 1) * 512],
                                        start=(d == 0), stop=(d == DC - 1))
                                sP.inc(ins)

                    @blk.vector
                    def _(ve):
                        ve.wait_ge(sQ.h, q0 + 1)
                        for i in range(4):
                            ve.tensor_copy(bqv[:, i * 512:(i + 1) * 512], pQ2[i][:])
                        sV.inc(ve.nop())
                        for vg in range(4):
                            for tt in range(TT):
                                k = vg * TT + tt
                                ve.wait_ge(sP.h, p0 + k + 1)
                                if k >= 4:
                                    ve.wait_ge(ST.h, st0 + 16 * (k - 3))
                                ins = ve.tensor_add(ev2[k % 4][:], pQ2[k % 4][:],
                                                    bqv[:, vg * 512:(vg + 1) * 512])
                                sW.inc(ins)

        if "P2" in phases:
            fence("f2")

        if "P3" in phases:
            with ExitStack() as ps:
                wbfs = [ps.enter_context(nc.sbuf_tensor(f"p3b{i}", [128, D], BF16)) for i in range(4)]
                silu_t = [ps.enter_context(nc.sbuf_tensor(f"p3s{i}", [128, 512], F32)) for i in range(2)]
                up_t = [ps.enter_context(nc.sbuf_tensor(f"p3u{i}", [128, 512], F32)) for i in range(2)]
                gu_t = [ps.enter_context(nc.sbuf_tensor(f"p3g{i}", [128, 512], BF16)) for i in range(4)]
                pG = [ps.enter_context(nc.psum_tensor(f"p3pg{i}", [128, 512], F32)) for i in range(2)]
                pU = [ps.enter_context(nc.psum_tensor(f"p3pu{i}", [128, 512], F32)) for i in range(2)]

                ld0, st0, a0, b0, p0, v0, w0 = LD.n, ST.n, sA.n, sB.n, sP.n, sV.n, sW.n
                with nc.Block("P3") as blk:
                    @blk.gpsimd
                    def _(gp):
                        for hc in range(HC):
                            if hc >= 2:
                                gp.wait_ge(sP.h, p0 + 8 * (hc - 1))
                            gp.dma_start(
                                out=wbfs[(2 * hc) % 4][:].rearrange("p (c h) -> p c h", c=DC),
                                in_=wgT[:, hc * 128:(hc + 1) * 128].rearrange(
                                    "(c p) h -> p c h", p=128)).then_inc(LD.h, 16)
                            gp.dma_start(
                                out=wbfs[(2 * hc + 1) % 4][:].rearrange("p (c h) -> p c h", c=DC),
                                in_=wuT[:, hc * 128:(hc + 1) * 128].rearrange(
                                    "(c p) h -> p c h", p=128)).then_inc(LD.h, 16)
                    LD.n += 32 * HC

                    @blk.scalar
                    def _(sc):
                        def silus(hc):
                            for g in range(8):
                                k = hc * 8 + g
                                sc.wait_ge(sP.h, p0 + k + 1)
                                if k >= 2:
                                    sc.wait_ge(sW.h, w0 + k - 1)
                                ins = sc.activation(out=silu_t[k % 2][:], in_=pG[k % 2][:],
                                                    func=AF.Silu, scale=ssg[:, hc:hc + 1])
                                sB.inc(ins)
                        for hc in range(HC):
                            sc.wait_ge(LD.h, ld0 + 16 * (2 * hc + 2))
                            ins = sc.activation(out=wbfs[(2 * hc) % 4][:],
                                                in_=wbfs[(2 * hc) % 4][:],
                                                func=AF.Copy, bias=-128.0)
                            sA.inc(ins)
                            ins = sc.activation(out=wbfs[(2 * hc + 1) % 4][:],
                                                in_=wbfs[(2 * hc + 1) % 4][:],
                                                func=AF.Copy, bias=-128.0)
                            sA.inc(ins)
                            if hc >= 1:
                                silus(hc - 1)
                        silus(HC - 1)

                    @blk.tensor
                    def _(te):
                        for hc in range(HC):
                            te.wait_ge(sA.h, a0 + 2 * hc + 2)
                            for g in range(8):
                                k = hc * 8 + g
                                if k >= 2:
                                    te.wait_ge(sB.h, b0 + k - 1)
                                    te.wait_ge(sV.h, v0 + k - 1)
                                for d in range(DC):
                                    te.matmul(pG[k % 2][:],
                                              wbfs[(2 * hc) % 4][:, d * 128:(d + 1) * 128],
                                              xn_fm[:, d * NT + g * 512: d * NT + g * 512 + 512],
                                              start=(d == 0), stop=(d == DC - 1))
                                for d in range(DC):
                                    ins = te.matmul(
                                        pU[k % 2][:],
                                        wbfs[(2 * hc + 1) % 4][:, d * 128:(d + 1) * 128],
                                        xn_fm[:, d * NT + g * 512: d * NT + g * 512 + 512],
                                        start=(d == 0), stop=(d == DC - 1))
                                sP.inc(ins)

                    @blk.vector
                    def _(ve):
                        for hc in range(HC):
                            for g in range(8):
                                k = hc * 8 + g
                                ve.wait_ge(sP.h, p0 + k + 1)
                                if k >= 2:
                                    ve.wait_ge(sW.h, w0 + k - 1)
                                ins = ve.tensor_scalar(up_t[k % 2][:], pU[k % 2][:],
                                                       ssu[:, hc:hc + 1], None, ALU.mult)
                                sV.inc(ins)
                                ve.wait_ge(sB.h, b0 + k + 1)
                                if k >= 4:
                                    ve.wait_ge(ST.h, st0 + 16 * (k - 3))
                                ins = ve.tensor_mul(gu_t[k % 4][:], silu_t[k % 2][:],
                                                    up_t[k % 2][:])
                                sW.inc(ins)

                    @blk.sync
                    def _(sy):
                        for hc in range(HC):
                            for g in range(8):
                                k = hc * 8 + g
                                sy.wait_ge(sW.h, w0 + k + 1)
                                sy.dma_start(out=gu_s[hc, :, g * 512:(g + 1) * 512],
                                             in_=gu_t[k % 4][:]).then_inc(ST.h, 16)
                    ST.n += 16 * 8 * HC

        if "P3" in phases:
            fence("f3")

        es_xn.close()

        if "P5" in phases:
            with ExitStack() as ps:
                mask_a = ps.enter_context(nc.sbuf_tensor("mask_a", [128, H * 128], BF16))
                qdec_s = ps.enter_context(nc.sbuf_tensor("qdec_s", [128, H * 128], F32))
                kdec_s = ps.enter_context(nc.sbuf_tensor("kdec_s", [128, H], F32))
                bd_s = ps.enter_context(nc.sbuf_tensor("bd_s", [128, H], F32))
                dtmp = ps.enter_context(nc.sbuf_tensor("dtmp", [128, 32], F32))
                dtmp2 = ps.enter_context(nc.sbuf_tensor("dtmp2", [128, H * 128], F32))

                sq_st = [ps.enter_context(nc.sbuf_tensor(f"p5q{i}", [128, H * 512], BF16)) for i in range(2)]
                sk_st = [ps.enter_context(nc.sbuf_tensor(f"p5k{i}", [128, H * 512], BF16)) for i in range(2)]
                sv_st = [ps.enter_context(nc.sbuf_tensor(f"p5v{i}", [128, 4 * D], BF16)) for i in range(2)]
                kdT = ps.enter_context(nc.sbuf_tensor("p5kd", [128, H * 128], BF16))
                msc = ps.enter_context(nc.sbuf_tensor("p5ms", [128, H * 128], BF16))
                qd_t = ps.enter_context(nc.sbuf_tensor("p5qd", [128, H * 512], BF16))
                o_st = [ps.enter_context(nc.sbuf_tensor(f"p5o{i}", [128, H * 512], BF16)) for i in range(2)]

                pSO = ps.enter_context(nc.psum_tensor("p5so", [128, 2048], F32))
                pT5 = ps.enter_context(nc.psum_tensor("p5t", [128, 2048], BF16))
                pKV = ps.enter_context(nc.psum_tensor("p5kv", [128, 1024], F32))

                slope_v = sslope[:].rearrange("p (h o) -> p h o", o=1).broadcast_to(
                    [128, H, 128])
                g0, a0, v0 = sG.n, sA.n, sV.n
                with nc.Block("P5s") as blk:
                    @blk.gpsimd
                    def _(gp):
                        ins = gp.iota(dtmp2[:].rearrange("p (h i) -> p h i", h=H),
                                      [[0, H], [-1, 128]], base=0, channel_multiplier=1,
                                      allow_small_or_imprecise_dtypes=True)
                        sG.inc(ins)

                    @blk.vector
                    def _(ve):
                        ve.wait_ge(sG.h, g0 + 1)
                        ins = ve.tensor_tensor(dtmp2[:].rearrange("p (h i) -> p h i", h=H),
                                               dtmp2[:].rearrange("p (h i) -> p h i", h=H),
                                               slope_v, ALU.mult)
                        sV.inc(ins)
                        ve.memset(S_f[:], 0.0)
                        ve.memset(S_b[:], 0.0)

                    @blk.scalar
                    def _(sc):
                        sc.wait_ge(sV.h, v0 + 1)
                        ins = sc.activation(out=mask_a[:], in_=dtmp2[:], func=AF.Exp)
                        sA.inc(ins)

                g1, a1, v1 = sG.n, sA.n, sV.n
                with nc.Block("P5s2") as blk:
                    @blk.gpsimd
                    def _(gp):
                        gp.affine_select(
                            mask_a[:].rearrange("p (h i) -> p h i", h=H),
                            mask_a[:].rearrange("p (h i) -> p h i", h=H),
                            [[0, H], [1, 128]], ALU.is_ge, 0.0,
                            base=0, channel_multiplier=-1)
                        gp.iota(dtmp2[:].rearrange("p (h i) -> p h i", h=H),
                                [[0, H], [-1, 128]], base=0, channel_multiplier=0,
                                allow_small_or_imprecise_dtypes=True)
                        ins = gp.iota(dtmp[:, 0:1], [[0, 1]], base=-128, channel_multiplier=1,
                                      allow_small_or_imprecise_dtypes=True)
                        sG.inc(ins)

                    @blk.vector
                    def _(ve):
                        ve.wait_ge(sG.h, g1 + 1)
                        ve.tensor_tensor(dtmp2[:].rearrange("p (h i) -> p h i", h=H),
                                         dtmp2[:].rearrange("p (h i) -> p h i", h=H),
                                         slope_v, ALU.mult)
                        ins = ve.tensor_tensor(dtmp[:, 16:16 + H],
                                               dtmp[:, 0:1].broadcast_to([128, H]),
                                               sslope[:], ALU.mult)
                        sV.inc(ins)

                    @blk.scalar
                    def _(sc):
                        sc.wait_ge(sV.h, v1 + 1)
                        sc.activation(out=qdec_s[:], in_=dtmp2[:], func=AF.Exp)
                        sc.activation(out=kdec_s[:], in_=dtmp[:, 16:16 + H], func=AF.Exp)
                        ins = sc.activation(out=bd_s[:], in_=sslope[:], func=AF.Exp,
                                            scale=-128.0)
                        sA.inc(ins)

                ld0, st0 = LD.n, ST.n
                a0, p0, v0 = sA.n, sP.n, sV.n
                def peb(t):
                    return p0 + 7 * t

                def dvb(t):
                    return v0 + 5 * t + (t // 4 + 1)

                def acb(t):
                    return a0 + 3 * t

                bd_v = bd_s[:].rearrange("p (h o) -> p h o", o=1).broadcast_to([128, H, 128])
                kd_v = kdec_s[:].rearrange("p (h o) -> p h o", o=1).broadcast_to([128, H, 128])
                qd_v = qdec_s[:].rearrange("p (h o i) -> p h o i", h=H, o=1).broadcast_to(
                    [128, H, 4, 128])

                with nc.Block("P5m") as blk:
                    @blk.sync
                    def _(sy):
                        for gr in range(GR):
                            if gr >= 2:
                                sy.wait_ge(sP.h, peb(4 * (gr - 1)))
                            for h in range(H):
                                sy.dma_start(out=sq_st[gr % 2][:, h * 512:(h + 1) * 512],
                                             in_=q_s[h, :, gr * 512:(gr + 1) * 512]).then_inc(LD.h, 16)
                                sy.dma_start(out=sk_st[gr % 2][:, h * 512:(h + 1) * 512],
                                             in_=k_s[h, :, gr * 512:(gr + 1) * 512]).then_inc(LD.h, 16)
                            sy.dma_start(
                                out=sv_st[gr % 2][:].rearrange("p (c n) -> p c n", c=4),
                                in_=v_s2[gr * 512:(gr + 1) * 512, :].rearrange(
                                    "(c p) n -> p c n", p=128)).then_inc(LD.h, 16)
                    LD.n += 16 * 33 * GR

                    @blk.tensor
                    def _(te):
                        for t in range(CH):
                            gr, c = t // 4, t % 4
                            qo = gr % 2

                            def vsl(h):
                                return sv_st[qo][:, c * D + h * 128: c * D + h * 128 + 128]
                            if c == 0:
                                te.wait_ge(LD.h, ld0 + 16 * 33 * (gr + 1))
                            if t >= 1:
                                te.wait_ge(sV.h, dvb(t - 1) + 1)
                            for h in range(H):
                                ins = te.transpose(
                                    pT5[:, h * 128:(h + 1) * 128],
                                    sk_st[qo][:, h * 512 + c * 128: h * 512 + c * 128 + 128],
                                    sid[:])
                            sP.inc(ins)
                            for half in range(2):
                                if t >= 1:
                                    te.wait_ge(sA.h, acb(t - 1) + 1 + 2 * half)
                                for h in range(8 * half, 8 * half + 8):
                                    ins = te.matmul(
                                        pSO[:, h * 128:(h + 1) * 128],
                                        sk_st[qo][:, h * 512 + c * 128: h * 512 + c * 128 + 128],
                                        sq_st[qo][:, h * 512 + c * 128: h * 512 + c * 128 + 128],
                                        start=True, stop=True)
                                sP.inc(ins)
                            te.wait_ge(sV.h, dvb(t) + 1)
                            if t >= 1:
                                te.wait_ge(sV.h, dvb(t - 1) + 5)
                            for h in range(8):
                                ins = te.matmul(pKV[:, h * 128:(h + 1) * 128],
                                                kdT[:, h * 128:(h + 1) * 128],
                                                vsl(h), start=True, stop=True)
                            sP.inc(ins)
                            te.wait_ge(sV.h, dvb(t) + 2)
                            if t >= 1:
                                te.wait_ge(sA.h, acb(t - 1) + 2)
                            for h in range(8):
                                te.matmul(pSO[:, h * 128:(h + 1) * 128],
                                          vsl(h),
                                          msc[:, h * 128:(h + 1) * 128],
                                          start=True, stop=False)
                                ins = te.matmul(
                                    pSO[:, h * 128:(h + 1) * 128],
                                    S_b[:, h * 128:(h + 1) * 128],
                                    qd_t[:, h * 512 + c * 128: h * 512 + c * 128 + 128],
                                    start=False, stop=True)
                            sP.inc(ins)
                            te.wait_ge(sV.h, dvb(t) + 4)
                            for h in range(8, 16):
                                ins = te.matmul(pKV[:, (h - 8) * 128:(h - 7) * 128],
                                                kdT[:, h * 128:(h + 1) * 128],
                                                vsl(h), start=True, stop=True)
                            sP.inc(ins)
                            te.wait_ge(sV.h, dvb(t) + 3)
                            for h in range(8, 16):
                                te.matmul(pSO[:, h * 128:(h + 1) * 128],
                                          vsl(h),
                                          msc[:, h * 128:(h + 1) * 128],
                                          start=True, stop=False)
                                ins = te.matmul(
                                    pSO[:, h * 128:(h + 1) * 128],
                                    S_b[:, h * 128:(h + 1) * 128],
                                    qd_t[:, h * 512 + c * 128: h * 512 + c * 128 + 128],
                                    start=False, stop=True)
                            sP.inc(ins)

                    @blk.vector
                    def _(ve):
                        for t in range(CH):
                            gr, c = t // 4, t % 4
                            if c == 0:
                                ve.wait_ge(LD.h, ld0 + 16 * 33 * (gr + 1))
                                if gr >= 1:
                                    ve.wait_ge(sP.h, peb(4 * gr))
                                ins = ve.tensor_tensor(
                                    qd_t[:].rearrange("p (h o i) -> p h o i", h=H, o=4),
                                    sq_st[gr % 2][:].rearrange("p (h o i) -> p h o i", h=H, o=4),
                                    qd_v, ALU.mult)
                                sV.inc(ins)
                            ve.wait_ge(sP.h, peb(t) + 1)
                            if t >= 1:
                                ve.wait_ge(sP.h, peb(t - 1) + 6)
                            ins = ve.tensor_tensor(
                                kdT[:].rearrange("p (h i) -> p h i", h=H),
                                pT5[:].rearrange("p (h i) -> p h i", h=H),
                                kd_v, ALU.mult)
                            sV.inc(ins)
                            ve.wait_ge(sP.h, peb(t) + 2)
                            if t >= 1:
                                ve.wait_ge(sP.h, peb(t - 1) + 5)
                            ins = ve.tensor_mul(msc[:, 0:1024], pSO[:, 0:1024],
                                                mask_a[:, 0:1024])
                            sV.inc(ins)
                            ve.wait_ge(sP.h, peb(t) + 3)
                            if t >= 1:
                                ve.wait_ge(sP.h, peb(t - 1) + 7)
                            ins = ve.tensor_mul(msc[:, 1024:2048], pSO[:, 1024:2048],
                                                mask_a[:, 1024:2048])
                            sV.inc(ins)
                            ve.wait_ge(sP.h, peb(t) + 4)
                            if t >= 1:
                                ve.wait_ge(sA.h, acb(t - 1) + 2)
                            ve.tensor_tensor(
                                S_f[:, 0:1024].rearrange("p (h i) -> p h i", h=8),
                                S_f[:, 0:1024].rearrange("p (h i) -> p h i", h=8),
                                bd_v[:, 0:8], ALU.mult)
                            ins = ve.tensor_add(S_f[:, 0:1024], S_f[:, 0:1024], pKV[:])
                            sV.inc(ins)
                            ve.wait_ge(sP.h, peb(t) + 6)
                            ve.tensor_tensor(
                                S_f[:, 1024:2048].rearrange("p (h i) -> p h i", h=8),
                                S_f[:, 1024:2048].rearrange("p (h i) -> p h i", h=8),
                                bd_v[:, 8:16], ALU.mult)
                            ins = ve.tensor_add(S_f[:, 1024:2048], S_f[:, 1024:2048], pKV[:])
                            sV.inc(ins)

                    @blk.scalar
                    def _(sc):
                        for t in range(CH):
                            gr, c = t // 4, t % 4
                            sc.wait_ge(sP.h, peb(t) + 5)
                            if t >= 8:
                                sc.wait_ge(ST.h, st0 + 16 * 16 * (gr - 1))
                            ins = sc.activation(
                                out=o_st[gr % 2][:].rearrange("p (h i) -> p h i", h=H)[
                                    :, 0:8, c * 128:(c + 1) * 128],
                                in_=pSO[:, 0:1024].rearrange("p (h i) -> p h i", h=8),
                                func=AF.Copy)
                            sA.inc(ins)
                            sc.wait_ge(sV.h, dvb(t) + 5)
                            sc.wait_ge(sP.h, peb(t) + 7)
                            ins = sc.activation(out=S_b[:], in_=S_f[:], func=AF.Copy)
                            sA.inc(ins)
                            ins = sc.activation(
                                out=o_st[gr % 2][:].rearrange("p (h i) -> p h i", h=H)[
                                    :, 8:16, c * 128:(c + 1) * 128],
                                in_=pSO[:, 1024:2048].rearrange("p (h i) -> p h i", h=8),
                                func=AF.Copy)
                            sA.inc(ins)

                    @blk.gpsimd
                    def _(gp):
                        for gr in range(GR):
                            gp.wait_ge(sA.h, acb(gr * 4 + 3) + 3)
                            for h in range(H):
                                gp.dma_start(out=a_s[h, :, gr * 512:(gr + 1) * 512],
                                             in_=o_st[gr % 2][:, h * 512:(h + 1) * 512]
                                             ).then_inc(ST.h, 16)
                    ST.n += 16 * 16 * GR

                stc = ST.n
                with nc.Block("P5c") as blk:
                    @blk.gpsimd
                    def _(gp):
                        gp.wait_ge(sV.h, sV.n)
                        gp.dma_start(out=cc_in.rearrange("(h d) e -> d h e", d=128),
                                     in_=S_f[:].rearrange("p (h e) -> p h e", h=H)
                                     ).then_inc(ST.h, 16)
                        gp.wait_ge(ST.h, stc + 16)
                        gp.collective_compute(
                            "AllGather", ALU.bypass,
                            replica_groups=[list(range(N_CORES))],
                            ins=[cc_in.opt()], outs=[cc_out.opt()],
                        ).then_inc(CC.h, 1)
                ST.n += 16
                CC.n += 1

        if "P5" in phases:
            fence("f5", cc=True)

        if "P4" in phases:
            NQ = 4
            KG = 11
            with ExitStack() as ps:
                acc = [ps.enter_context(nc.sbuf_tensor(f"p4a{i}", [128, NT], F32)) for i in range(4)]
                gu_st = [[ps.enter_context(nc.sbuf_tensor(f"p4g{j}_{i}", [128, NT], BF16))
                          for i in range(4)] for j in range(2)]
                wd_b = [ps.enter_context(nc.sbuf_tensor(f"p4wb{i}", [128, 2048], BF16)) for i in range(2)]
                gsil = ps.enter_context(nc.sbuf_tensor("p4gs", [128, 4 * NT], BF16))
                gtm_t = [ps.enter_context(nc.sbuf_tensor(f"p4t{i}", [128, 2048], BF16)) for i in range(2)]
                pD = [ps.enter_context(nc.psum_tensor(f"p4p{i}", [128, 512], F32)) for i in range(4)]
                pTr = [ps.enter_context(nc.psum_tensor(f"p4q{i}", [128, 512], BF16)) for i in range(4)]

                ld0, st0, a0, b0, p0, q0, v0, w0 = LD.n, ST.n, sA.n, sB.n, sP.n, sQ.n, sV.n, sW.n
                g4 = sG.n
                def mmi(dq, kg, dc, g):
                    return p0 + ((dq * KG + kg) * 4 + dc) * 8 + g + 1

                def addi(dq, kg, dc, g):
                    return v0 + ((dq * KG + kg) * 4 + dc) * 8 + g + 1

                gsv = gsil[:].rearrange("p (dc n) -> p dc n", dc=4)

                with nc.Block("P4") as blk:
                    @blk.sync
                    def _(sy):
                        for dq in range(NQ):
                            for kg in range(KG):
                                u = dq * KG + kg
                                if u >= 2:
                                    dqp, kgp = divmod(u - 2, KG)
                                    sy.wait_ge(sP.h, mmi(dqp, kgp, 3, 7))
                                for i in range(4):
                                    sy.dma_start(out=gu_st[u % 2][i][:],
                                                 in_=gu_s[kg * 4 + i, :, :]).then_inc(LD.h, 16)
                    LD.n += 16 * 4 * NQ * KG

                    @blk.scalar
                    def _(sc):
                        def gevac(dqq):
                            for tile in range(TT):
                                k6 = dqq * TT + tile
                                sc.wait_ge(sQ.h, q0 + k6 + 1)
                                if k6 >= 8:
                                    sc.wait_ge(ST.h, st0 + 16 * (k6 // 4 - 1))
                                ins = sc.activation(
                                    out=gtm_t[(tile // 4) % 2][:, (tile % 4) * 512:
                                                               (tile % 4) * 512 + 512],
                                    in_=pTr[tile % 4][:], func=AF.Copy)
                                sW.inc(ins)
                                if tile % 4 == 3:
                                    sc.wait_ge(sW.h, sW.n)
                                    sc.dma_start(
                                        out=g_tm[(tile - 3) * 128:(tile + 1) * 128,
                                                 dqq * 512:(dqq + 1) * 512].rearrange(
                                                     "(c p) n -> p c n", p=128),
                                        in_=gtm_t[(tile // 4) % 2][:].rearrange(
                                            "p (c n) -> p c n", c=4)).then_inc(ST.h, 16)
                        for dq in range(NQ):
                            for kg in range(KG):
                                u = dq * KG + kg
                                sc.wait_ge(sG.h, g4 + 16 * (u + 1))
                                ins = sc.activation(out=wd_b[u % 2][:], in_=wd_b[u % 2][:],
                                                    func=AF.Copy, bias=-128.0)
                                sA.inc(ins)
                            if dq >= 1:
                                gevac(dq - 1)
                            sc.wait_ge(sV.h, addi(dq, KG - 1, 3, 7))
                            if dq >= 1:
                                sc.wait_ge(sQ.h, q0 + dq * TT)
                            for dc in range(4):
                                ins = sc.activation(out=gsv[:, dc, :], in_=acc[dc][:],
                                                    func=AF.Silu,
                                                    scale=ssd[:, dq * 4 + dc:dq * 4 + dc + 1])
                            sB.inc(ins)
                        gevac(NQ - 1)

                    @blk.tensor
                    def _(te):
                        def trans_quarter(dq):
                            te.wait_ge(sB.h, b0 + dq + 1)
                            for tile in range(TT):
                                if dq * TT + tile >= 4:
                                    te.wait_ge(sW.h, w0 + dq * TT + tile - 3)
                                for dc in range(4):
                                    ins = te.transpose(
                                        pTr[tile % 4][:, dc * 128:(dc + 1) * 128],
                                        gsv[:, dc, tile * 128:(tile + 1) * 128],
                                        sid[:])
                                sQ.inc(ins)
                        for dq in range(NQ):
                            for kg in range(KG):
                                u = dq * KG + kg
                                te.wait_ge(sA.h, a0 + u + 1)
                                te.wait_ge(LD.h, ld0 + 16 * 4 * (u + 1))
                                for dc in range(4):
                                    for g in range(8):
                                        j = (dc * 8 + g) % 4
                                        if u > 0 or dc * 8 + g >= 4:
                                            te.wait_ge(sV.h, addi(dq, kg, dc, g) - 4)
                                        for i in range(4):
                                            ins = te.matmul(
                                                pD[j][:],
                                                wd_b[u % 2][:, i * 512 + dc * 128:
                                                             i * 512 + dc * 128 + 128],
                                                gu_st[u % 2][i][:, g * 512:(g + 1) * 512],
                                                start=(i == 0), stop=(i == 3))
                                        sP.inc(ins)
                            if dq >= 1:
                                trans_quarter(dq - 1)
                        trans_quarter(NQ - 1)

                    @blk.vector
                    def _(ve):
                        for dq in range(NQ):
                            for kg in range(KG):
                                for dc in range(4):
                                    for g in range(8):
                                        j = (dc * 8 + g) % 4
                                        ve.wait_ge(sP.h, mmi(dq, kg, dc, g))
                                        if kg == 0:
                                            if dq >= 1:
                                                ve.wait_ge(sB.h, b0 + dq)
                                            ins = ve.tensor_copy(
                                                acc[dc][:, g * 512:(g + 1) * 512], pD[j][:])
                                        else:
                                            ins = ve.tensor_add(
                                                acc[dc][:, g * 512:(g + 1) * 512],
                                                acc[dc][:, g * 512:(g + 1) * 512], pD[j][:])
                                        sV.inc(ins)


                    @blk.gpsimd
                    def _(gp):
                        for dq in range(NQ):
                            for kg in range(KG):
                                u = dq * KG + kg
                                if u >= 2:
                                    dqp, kgp = divmod(u - 2, KG)
                                    gp.wait_ge(sP.h, mmi(dqp, kgp, 3, 7))
                                gp.dma_start(
                                    out=wd_b[u % 2][:].rearrange("p (a n) -> p a n", a=4),
                                    in_=wdT.rearrange("(a p) n -> p a n", p=128)[
                                        :, kg * 4:kg * 4 + 4, dq * 512:(dq + 1) * 512],
                                ).then_inc(sG.h, 16)
                    sG.n += 16 * NQ * KG
                    ST.n += 16 * NQ * (TT // 4)

        if "P4" in phases:
            fence("f4")

        if "P55" in phases and "P6" in phases:
            with ExitStack() as ps:
                qde = ps.enter_context(nc.sbuf_tensor("p7qe", [128, H * 512], BF16))
                bd5 = ps.enter_context(nc.sbuf_tensor("p7bd", [128, H], F32))
                pS6 = [ps.enter_context(nc.psum_tensor(f"p7p{i}", [128, D], F32)) for i in range(2)]

                with ExitStack() as ps2:
                    dneg = ps2.enter_context(nc.sbuf_tensor("p7i", [128, 513], F32))
                    ld_r = ps2.enter_context(nc.sbuf_tensor("p7ld", [128, H * 128], F32))

                    g0, a00, v00 = sG.n, sA.n, sV.n
                    with nc.Block("P7s") as blk:
                        @blk.gpsimd
                        def _(gp):
                            gp.iota(dneg[:, 0:512], [[-1, 512]], base=0, channel_multiplier=0,
                                    allow_small_or_imprecise_dtypes=True)
                            ins = gp.memset(dneg[:, 512:513], -512.0)
                            sG.inc(ins)

                        @blk.scalar
                        def _(sc):
                            sc.wait_ge(sG.h, g0 + 1)
                            for h in range(H):
                                sc.activation(out=qde[:, h * 512:(h + 1) * 512],
                                              in_=dneg[:, 0:512],
                                              func=AF.Exp, scale=sslope[:, h:h + 1])
                                ins = sc.activation(out=bd5[:, h:h + 1], in_=dneg[:, 512:513],
                                                    func=AF.Exp, scale=sslope[:, h:h + 1])
                            sA.inc(ins)

                        @blk.vector
                        def _(ve):
                            ve.memset(S_f[:], 0.0)

                    ld0 = LD.n
                    v0 = sV.n
                    bd5_v = bd5[:].rearrange("p (h o) -> p h o", o=1).broadcast_to([128, H, 128])
                    with nc.Block("P7a") as blk:
                        @blk.sync
                        def _(sy):
                            sy.wait_ge(CC.h, CC.n)
                            for r in range(N_CORES):
                                if r >= 1:
                                    sy.wait_ge(sV.h, v0 + r)
                                sy.dma_start(
                                    out=ld_r[:].rearrange("p (h e) -> p h e", h=H),
                                    in_=cc_out.rearrange("(r h d) e -> r d h e",
                                                         r=N_CORES, d=128)[r],
                                ).then_inc(LD.h, 16)
                        LD.n += 16 * N_CORES

                        @blk.vector
                        def _(ve):
                            for r in range(N_CORES):
                                ve.wait_ge(LD.h, ld0 + 16 * (r + 1))
                                ins = ve.scalar_tensor_tensor(S_f[:], ld_r[:], ssel[:, r:r + 1],
                                                              S_f[:], ALU.mult, ALU.add)
                                sV.inc(ins)

                        @blk.scalar
                        def _(sc):
                            sc.wait_ge(sV.h, v0 + N_CORES)
                            ins = sc.activation(out=S_b[:], in_=S_f[:], func=AF.Copy)
                            sA.inc(ins)

                    ld0, st0, a0, p0, q0, v0, w0, d0, g6, c0 = (
                        LD.n, ST.n, sA.n, sP.n, sQ.n, sV.n, sW.n, sD.n, sG.n, sC.n)

                wo_b = ps.enter_context(nc.sbuf_tensor("p7wb", [128, DC * D], BF16))
                q5 = [ps.enter_context(nc.sbuf_tensor(f"p7q{i}", [128, H * 512], BF16)) for i in range(2)]
                at5 = [ps.enter_context(nc.sbuf_tensor(f"p7a{i}", [128, H * 512], BF16)) for i in range(2)]
                g_f = [ps.enter_context(nc.sbuf_tensor(f"p7gf{i}", [128, D], F32)) for i in range(2)]
                sgp = ps.enter_context(nc.sbuf_tensor("p7s", [128, D], F32))
                x_st6 = [ps.enter_context(nc.sbuf_tensor(f"p7x{i}", [128, D], F32)) for i in range(2)]

                with nc.Block("P7m") as blk:
                    @blk.sync
                    def _(sy):
                        for g in range(GR):
                            if g >= 2:
                                sy.wait_ge(sP.h, p0 + 4 * (g - 1))
                                sy.wait_ge(sQ.h, q0 + 4 * (g - 1))
                            for h in range(H):
                                sy.dma_start(out=at5[g % 2][:, h * 512:(h + 1) * 512],
                                             in_=a_s[h, :, g * 512:(g + 1) * 512]
                                             ).then_inc(LD.h, 16)
                                sy.dma_start(out=q5[g % 2][:, h * 512:(h + 1) * 512],
                                             in_=q_s[h, :, g * 512:(g + 1) * 512]
                                             ).then_inc(LD.h, 16)
                            for tl in range(4):
                                t = g * 4 + tl
                                if t >= 2:
                                    sy.wait_ge(ST.h, st0 + 16 * (t - 1))
                                sy.dma_start(out=x_st6[t % 2][:],
                                             in_=x_in[t * 128:(t + 1) * 128, :]
                                             ).then_inc(LD.h, 16)
                    LD.n += 16 * 36 * GR

                    @blk.gpsimd
                    def _(gp):
                        gp.dma_start(
                            out=wo_b[:].rearrange("p (c n) -> p c n", c=DC),
                            in_=wout.rearrange("(c p) n -> p c n", p=128),
                        ).then_inc(sG.h, 16)
                        for t in range(TT):
                            if t >= 2:
                                gp.wait_ge(sV.h, v0 + t - 1)
                            gp.dma_start(out=g_f[t % 2][:],
                                         in_=g_tm[t * 128:(t + 1) * 128, :]).then_inc(sG.h, 16)
                    sG.n += 16 * (TT + 1)

                    @blk.vector
                    def _(ve):
                        nocorr = "NOCORR" in phases
                        for g in range(GR):
                            ve.wait_ge(LD.h, ld0 + 16 * (36 * g + 32))
                            ins = ve.tensor_mul(q5[g % 2][:], q5[g % 2][:], qde[:])
                            sD.inc(ins)
                            for qq in range(4):
                                if nocorr:
                                    sW.inc(ve.nop())
                                    continue
                                ve.wait_ge(sQ.h, q0 + g * 4 + qq + 1)
                                ins = ve.tensor_add(at5[g % 2][:, qq * 2048:(qq + 1) * 2048],
                                                    pS6[qq % 2][:],
                                                    at5[g % 2][:, qq * 2048:(qq + 1) * 2048])
                                sW.inc(ins)
                            ve.wait_ge(sA.h, a0 + g)
                            ins = ve.tensor_tensor(
                                S_f[:].rearrange("p (h i) -> p h i", h=H),
                                S_f[:].rearrange("p (h i) -> p h i", h=H),
                                bd5_v, ALU.mult)
                            sD.inc(ins)
                            for tl in range(4):
                                t = g * 4 + tl
                                ve.wait_ge(sP.h, p0 + t + 1)
                                ve.wait_ge(sG.h, g6 + 16 * (t + 2))
                                ins = ve.tensor_mul(sgp[:], pS6[t % 2][:], g_f[t % 2][:])
                                sV.inc(ins)
                                ve.wait_ge(LD.h, ld0 + 16 * (36 * g + 32 + tl + 1))
                                ins = ve.tensor_add(x_st6[t % 2][:], sgp[:], x_st6[t % 2][:])
                                sC.inc(ins)

                    @blk.scalar
                    def _(sc):
                        for g in range(GR):
                            if g >= 1:
                                sc.wait_ge(sD.h, d0 + 2 * g)
                                sc.wait_ge(sQ.h, q0 + 4 * g)
                                ins = sc.activation(out=S_b[:], in_=S_f[:], func=AF.Copy)
                                sA.inc(ins)
                            for tl in range(4):
                                t = g * 4 + tl
                                sc.wait_ge(sC.h, c0 + t + 1)
                                sc.dma_start(out=out[t * 128:(t + 1) * 128, :],
                                             in_=x_st6[t % 2][:]).then_inc(ST.h, 16)
                    ST.n += 16 * TT

                    @blk.tensor
                    def _(te):
                        te.wait_ge(sG.h, g6 + 16)
                        for g in range(GR):
                            te.wait_ge(sD.h, d0 + 2 * g + 1)
                            te.wait_ge(sA.h, a0 + g)
                            if g >= 1:
                                te.wait_ge(sV.h, v0 + 4 * g)
                            for qq in range(4):
                                if "NOCORR" in phases:
                                    sQ.inc(te.nop())
                                    continue
                                if qq >= 2:
                                    te.wait_ge(sW.h, w0 + 4 * g + qq - 1)
                                for hh in range(4):
                                    h = qq * 4 + hh
                                    ins = te.matmul(pS6[qq % 2][:, hh * 512:(hh + 1) * 512],
                                                    S_b[:, h * 128:(h + 1) * 128],
                                                    q5[g % 2][:, h * 512:(h + 1) * 512],
                                                    start=True, stop=True)
                                sQ.inc(ins)
                            te.wait_ge(sW.h, w0 + 4 * (g + 1))
                            for tl in range(4):
                                t = g * 4 + tl
                                if t >= 2:
                                    te.wait_ge(sV.h, v0 + t - 1)
                                for nn in range(4):
                                    for cc in range(DC):
                                        te.matmul(
                                            pS6[t % 2][:, nn * 512:(nn + 1) * 512],
                                            at5[g % 2][:, cc * 512 + tl * 128:
                                                         cc * 512 + tl * 128 + 128],
                                            wo_b[:, cc * D + nn * 512: cc * D + nn * 512 + 512],
                                            start=(cc == 0), stop=False)
                                    ins = te.matmul(pS6[t % 2][:, nn * 512:(nn + 1) * 512],
                                                    sones[:, 0:128],
                                                    sbout_b[:, nn * 512:(nn + 1) * 512],
                                                    start=False, stop=True)
                                sP.inc(ins)

        with nc.Block("drain") as blk:
            @blk.gpsimd
            def _(gp):
                gp.wait_ge(ST.h, ST.n)
            @blk.sync
            def _(sy):
                sy.wait_ge(LD.h, LD.n)

    return nc


def _prep_inputs(L0_attn_x, L0_attn_Wqkv, L0_attn_bqkv, L0_attn_Wout, L0_attn_bout,
                 L0_attn_slopes, L0_ffn_wq_gate, L0_ffn_s_gate, L0_ffn_wq_up,
                 L0_ffn_s_up, L0_ffn_wq_down, L0_ffn_s_down):
    x = np.ascontiguousarray(np.asarray(L0_attn_x, dtype=np.float32)).reshape(4 * 8192, D)
    shared = {
        "wqkv": np.ascontiguousarray(np.asarray(L0_attn_Wqkv, dtype=np.float32)),
        "bqkv_c": np.ascontiguousarray(np.asarray(L0_attn_bqkv, np.float32).reshape(48, 128).T),
        "wout": np.ascontiguousarray(np.asarray(L0_attn_Wout, dtype=np.float32)),
        "bout_r": np.ascontiguousarray(np.asarray(L0_attn_bout, np.float32).reshape(1, D)),
        "slope_b": np.ascontiguousarray(
            np.broadcast_to(np.asarray(L0_attn_slopes, np.float32)[None, :], (128, H))),
        "wgT": np.ascontiguousarray(np.asarray(L0_ffn_wq_gate, np.int32).T),
        "wuT": np.ascontiguousarray(np.asarray(L0_ffn_wq_up, np.int32).T),
        "wdT": np.ascontiguousarray(np.asarray(L0_ffn_wq_down, np.int32).T),
        "sg_c": np.ascontiguousarray(np.asarray(L0_ffn_s_gate, np.float32).reshape(HC, 128).T),
        "su_c": np.ascontiguousarray(np.asarray(L0_ffn_s_up, np.float32).reshape(HC, 128).T),
        "sd_c": np.ascontiguousarray(np.asarray(L0_ffn_s_down, np.float32).reshape(DC, 128).T),
        "ident": np.eye(128, dtype=ml_dtypes.bfloat16),
        "ones_b": np.ones((1, 128), dtype=ml_dtypes.bfloat16),
        "ones_f": np.ones((1, 128), dtype=np.float32),
        "bqv_row": np.ascontiguousarray(
            np.asarray(L0_attn_bqkv, np.float32)[2 * D:].reshape(1, D)),
    }
    in_maps = []
    for c in range(N_CORES):
        sel = np.zeros((128, N_CORES), dtype=np.float32)
        if c % 2 == 1:
            sel[:, c - 1] = 1.0
        m = dict(shared)
        m["x"] = x[c * NT:(c + 1) * NT]
        m["selm"] = sel
        in_maps.append(m)
    return in_maps


def kernel(**inputs):
    in_maps = _prep_inputs(**inputs)
    nc = build_program()
    res = run_bass_kernel_spmd(nc, in_maps, core_ids=list(range(N_CORES)))
    outs = [r["out"] for r in res.results]
    return np.concatenate(outs, axis=0).reshape(4, 8192, D)
